# version 12
# baseline (speedup 1.0000x reference)
import contextlib
import numpy as np

import concourse.bass as bass
import concourse.mybir as mybir
from concourse.bass_utils import run_bass_kernel_spmd

F32 = mybir.dt.float32
BF16 = mybir.dt.bfloat16
I32 = mybir.dt.int32
AF = mybir.ActivationFunctionType
ALU = mybir.AluOpType
AX = mybir.AxisListType

D = 1024
KD = 8
E = 8
CH = 512
L = 256
NH = 16
II = 2048
LN_EPS = 1e-5


def _flat(waits):
    out = []
    for w in waits:
        if w is None:
            continue
        if isinstance(w, list):
            out.extend(x for x in w if x is not None)
        else:
            out.append(w)
    return out


class Em:
    ENGS = ("pe", "v", "s", "gp", "sy")

    def __init__(self, nc, ctx, sem_names):
        self.nc = nc
        self.q = {k: [] for k in self.ENGS}
        self.sem = {}
        self.cnt = {}
        self.waited = {k: {} for k in self.ENGS}
        for name in sem_names:
            self.sem[name] = ctx.enter_context(nc.semaphore(name))
            self.cnt[name] = 0

    def mark(self, sem):
        return (sem, self.cnt[sem])

    def ewait(self, eng, mark):
        if mark is None:
            return
        sem, val = mark
        if val <= 0:
            return
        assert val <= self.cnt[sem], f"wait on future: {eng} {sem} {val}>{self.cnt[sem]}"
        if self.waited[eng].get(sem, 0) >= val:
            return
        self.waited[eng][sem] = val
        s = self.sem[sem]
        self.q[eng].append(lambda e, s=s, val=val: e.wait_ge(s, val))

    def step(self, eng, fns, sem=None, inc=1, waits=(), inc_each=False):
        for m in _flat(waits):
            self.ewait(eng, m)
        if not isinstance(fns, (list, tuple)):
            fns = [fns]
        assert fns
        for i, fn in enumerate(fns):
            if sem is not None and (inc_each or i == len(fns) - 1):
                s = self.sem[sem]
                self.q[eng].append(
                    lambda e, fn=fn, s=s, inc=inc: fn(e).then_inc(s, inc))
                self.cnt[sem] += inc
            else:
                self.q[eng].append(lambda e, fn=fn: fn(e))
        if sem is not None:
            return (sem, self.cnt[sem])
        return None

    def fence(self, engs=("pe", "v", "s"), dma=()):
        return [self.mark(k) for k in engs] + [self.mark(d) for d in dma]


class BankPool:

    def __init__(self, em, tensors):
        self.banks = [{"t": t, "free": None} for t in tensors]
        self.i = 0

    def acquire(self):
        b = self.banks[self.i % len(self.banks)]
        self.i += 1
        return b

    @staticmethod
    def release(b, mark):
        b["free"] = mark


def build_program(NSEQ=4, T=2048, NS=2048):
    TT = NSEQ * T
    NS = min(NS, TT)
    NST = TT // NS
    NCH = NS // CH
    NTT = CH // 128

    nc = bass.Bass("TRN2", target_bir_lowering=False, debug=False)
    dtn = lambda n, sh, ty, kind: nc.dram_tensor(n, sh, ty, kind=kind).ap()
    X0 = dtn("X0", [D, TT], F32, "ExternalInput")
    OUT = dtn("OUT", [D, TT], F32, "ExternalOutput")
    XB1 = dtn("XB1", [D, TT], F32, "Internal")
    XB2 = dtn("XB2", [D, TT], F32, "Internal")
    W1 = dtn("W1", [E, D, D], F32, "ExternalInput")
    W2 = dtn("W2", [E, D, D], F32, "ExternalInput")
    W3 = dtn("W3", [E, D, D], F32, "ExternalInput")
    SW1 = dtn("SW1", [D, II], F32, "ExternalInput")
    SW2 = dtn("SW2", [D, II], F32, "ExternalInput")
    SW3 = dtn("SW3", [II, D], F32, "ExternalInput")
    B1 = dtn("B1", [128, KD, E], F32, "ExternalInput")
    B2 = dtn("B2", [128, KD, E], F32, "ExternalInput")
    B3 = dtn("B3", [128, KD, E], F32, "ExternalInput")
    GW = dtn("GW", [D, E], F32, "ExternalInput")
    WDQ = dtn("WDQ", [L, D], F32, "ExternalInput")
    WDQT = dtn("WDQT", [D, L], F32, "ExternalInput")
    WUQT = dtn("WUQT", [L, D], F32, "ExternalInput")
    WDKVT = dtn("WDKVT", [D, L], F32, "ExternalInput")
    WUK = dtn("WUK", [D, L], F32, "ExternalInput")
    WUV = dtn("WUV", [D, L], F32, "ExternalInput")
    WOT = dtn("WOT", [D, D], F32, "ExternalInput")

    sems = ["pe", "v", "s", "gp", "w0", "w1", "w2", "xl", "xla", "xlb",
            "sta", "stb", "ld", "aw"]

    with contextlib.ExitStack() as CTX:
        EC = CTX.enter_context
        em = Em(nc, CTX, sems)

        ones_f = EC(nc.sbuf_tensor("ones_f", [128, 1], F32))
        ones_b = EC(nc.sbuf_tensor("ones_b", [128, 1], BF16))
        ones1b = EC(nc.sbuf_tensor("ones1b", [1, 128], BF16))
        ones11f = EC(nc.sbuf_tensor("ones11f", [1, 1], F32))
        ones1f = EC(nc.sbuf_tensor("ones1f", [1, 128], F32))
        eps11 = EC(nc.sbuf_tensor("eps11", [1, 1], F32))
        onesc = EC(nc.sbuf_tensor("onesc", [128, 1], BF16))
        identb = EC(nc.sbuf_tensor("identb", [128, 128], BF16))
        identf = EC(nc.sbuf_tensor("identf", [128, 128], F32))
        sel_all = EC(nc.sbuf_tensor("sel_all", [8, E * 128], BF16))
        b1s = EC(nc.sbuf_tensor("b1s", [128, KD * E], F32))
        b2s = EC(nc.sbuf_tensor("b2s", [128, KD * E], F32))
        b3s = EC(nc.sbuf_tensor("b3s", [128, KD * E], F32))
        gw_sb = EC(nc.sbuf_tensor("gw_sb", [128, KD * E], F32))
        cs_sb = EC(nc.sbuf_tensor("cs_sb", [128, E], F32))

        psA = EC(nc.psum_tensor("psA", [128, 512], F32))
        psB = EC(nc.psum_tensor("psB", [128, 512], F32))
        ps0 = EC(nc.psum_tensor("ps0", [128, 512], F32))
        ps1 = EC(nc.psum_tensor("ps1", [128, 512], F32))
        ps2 = EC(nc.psum_tensor("ps2", [128, 512], F32))
        ps3 = EC(nc.psum_tensor("ps3", [128, 512], F32))
        psS = EC(nc.psum_tensor("psS", [128, 512], F32))
        psT = EC(nc.psum_tensor("psT", [8, 512], BF16))

        CSC = contextlib.ExitStack()
        jmp = CSC.enter_context(nc.sbuf_tensor("jmp", [128, 128], I32))
        selk = CSC.enter_context(nc.sbuf_tensor("selk", [8, E * 128], I32))
        m_gp = em.step("gp", [
            lambda e: e.memset(ones_f[:, :], 1.0 / D),
            lambda e: e.memset(ones_b[:, :], 1.0 / D),
            lambda e: e.memset(ones1b[:, :], 1.0),
            lambda e: e.memset(ones11f[:, :], 1.0),
            lambda e: e.memset(eps11[:, :], LN_EPS),
            lambda e: e.memset(ones1f[:, :], 1.0),
            lambda e: e.memset(onesc[:, :], 1.0),
            lambda e: e.iota(jmp[:, :], [[1, 128]], base=0, channel_multiplier=-1),
            lambda e: e.iota(selk[:, :], [[0, E * 128]], base=0,
                             channel_multiplier=1),
        ], sem="gp")
        m_ld = em.step("sy", [
            lambda e: e.dma_start(out=b1s[:, :].rearrange("p (k e) -> p k e", k=KD), in_=B1),
            lambda e: e.dma_start(out=b2s[:, :].rearrange("p (k e) -> p k e", k=KD), in_=B2),
            lambda e: e.dma_start(out=b3s[:, :].rearrange("p (k e) -> p k e", k=KD), in_=B3),
            lambda e: e.dma_start(out=gw_sb[:, :].rearrange("p (k e) -> p k e", k=KD),
                                  in_=GW.rearrange("(k p) e -> p k e", p=128)),
        ], sem="ld", inc=16, inc_each=True)
        m_cb = em.step("v", [
            lambda e: e.tensor_scalar(out=identb[:, :], in0=jmp[:, :], scalar1=0.0,
                                      scalar2=None, op0=ALU.is_equal),
        ], sem="v", waits=[m_gp])
        m_consts = em.step("v", [
            lambda e: e.tensor_copy(out=identf[:, :], in_=identb[:, :]),
        ] + [
            (lambda ee: (lambda e: e.tensor_scalar(
                out=sel_all[0:8, ee * 128:(ee + 1) * 128],
                in0=selk[0:8, ee * 128:(ee + 1) * 128],
                scalar1=float(ee), scalar2=None, op0=ALU.is_equal)))(ee)
            for ee in range(E)
        ], sem="v", waits=[m_gp, m_cb])
        CSC.close()
        m_cs0 = em.step("pe", [
            (lambda k: (lambda e: e.matmul(psS[0:1, 0:E],
                                           ones_f[:, :], gw_sb[:, k * E:(k + 1) * E],
                                           start=(k == 0), stop=(k == KD - 1))))(k)
            for k in range(KD)
        ], sem="pe", waits=[m_ld, m_gp, m_consts])
        m_cs1 = em.step("v", [
            lambda e: e.tensor_scalar(out=cs_sb[0:1, :], in0=psS[0:1, 0:E],
                                      scalar1=float(D), scalar2=None, op0=ALU.mult),
        ], sem="v", waits=[m_cs0])
        m_cs1b = em.step("pe", [
            lambda e: e.matmul(psS[:, 0:E], ones1f[0:1, :], cs_sb[0:1, :],
                               start=True, stop=True),
        ], sem="pe", waits=[m_cs1])
        m_cs2 = em.step("v", [
            lambda e: e.tensor_copy(out=cs_sb[:, :], in_=psS[:, 0:E]),
        ], sem="v", waits=[m_cs1b])

        wunits = {"i": 0, "free": [None, None, None]}

        def load_weight(src_ap, extra_waits=()):
            slot = wunits["i"] % 3
            wunits["i"] += 1
            dst = wslots[slot]
            m = em.step("gp", [
                lambda e: e.dma_start(out=dst[:, :].rearrange(
                    "p (k n) -> p k n", k=KD), in_=src_ap),
            ], sem=f"w{slot}", inc=16,
                waits=_flat([wunits["free"][slot]] + list(extra_waits)))
            return slot, m

        wslots = [None, None, None]

        def emit_ln_chunk(src_slice, xn_slice, routing, ssel_slice, B, prev_marks,
                          pool, c):
            par = c % len(B["x32c"])
            x32c, x2b = B["x32c"][par], B["x2b"][0]
            mu, var, rr = B["mu"][0], B["var"][par], B["rr"][0]
            rneg = var
            abf, bbf = B["abf"][0], B["bbf"][0]
            m_x = em.step("sy", [lambda e: e.dma_start(out=x32c[:, :].rearrange(
                "p (k n) -> p k n", k=KD), in_=src_slice)],
                sem="xla" if par == 0 else "xlb", inc=16,
                waits=prev_marks + B["free"][par])
            bmu = pool.acquire()
            bex = pool.acquire()
            m_ex2 = B.get("ex2_last")
            for hh in range(4):
                m_sqh = em.step("s", [
                    (lambda k, hh=hh: (lambda e: e.activation(
                        out=x2b[:, k * 128:(k + 1) * 128],
                        in_=x32c[:, k * CH + hh * 128:k * CH + hh * 128 + 128],
                        func=AF.Square)))(k)
                    for k in range(KD)
                ], sem="s", waits=[m_x, m_ex2])
                m_ex2 = em.step("pe", [
                    (lambda k, hh=hh, b=bex: (lambda e: e.matmul(
                        b["t"][0:1, hh * 128:(hh + 1) * 128], ones_b[:, :],
                        x2b[:, k * 128:(k + 1) * 128],
                        start=(k == 0), stop=(k == KD - 1))))(k)
                    for k in range(KD)
                ], sem="pe", waits=[m_sqh, m_gp, bex["free"] if hh == 0 else None])
            B["ex2_last"] = m_ex2
            B["mu_free"] = B.get("mu_free")
            m_mu = em.step("pe", [
                (lambda k, b=bmu: (lambda e: e.matmul(
                    b["t"][0:1, 0:CH], ones_f[:, :],
                    x32c[:, k * CH:(k + 1) * CH],
                    start=(k == 0), stop=(k == KD - 1))))(k)
                for k in range(KD)
            ], sem="pe", waits=[m_x, m_gp, bmu["free"]])
            m_graw = None
            if routing is not None:
                m_graw = em.step("pe", [
                    (lambda k, tt: (lambda e: e.matmul(
                        psS[:, c * 32 + tt * 8:c * 32 + tt * 8 + 8],
                        x32c[:, k * CH + tt * 128:k * CH + tt * 128 + 128],
                        gw_sb[:, k * E:(k + 1) * E],
                        start=(k == 0), stop=(k == KD - 1))))(k, tt)
                    for tt in range(NTT) for k in range(KD)
                ], sem="pe", waits=[m_x, m_ld, m_cs2])
            m_v1 = em.step("v", [(lambda b=bmu: (lambda e: e.tensor_copy(
                out=mu[0:1, :], in_=b["t"][0:1, 0:CH])))()],
                sem="v", waits=[m_mu, B.get("mu_free")])
            pool.release(bmu, m_v1)
            m_v2 = em.step("v", [lambda e: e.tensor_tensor(out=var[0:1, :], in0=mu[0:1, :],
                                                           in1=mu[0:1, :], op=ALU.mult)],
                           sem="v", waits=[m_v1])
            m_v3 = em.step("v", [(lambda b=bex: (lambda e: e.tensor_tensor(
                out=var[0:1, :], in0=b["t"][0:1, 0:CH],
                in1=var[0:1, :], op=ALU.subtract)))()],
                sem="v", waits=[m_v2, m_ex2])
            pool.release(bex, m_v3)
            m_s1 = em.step("s", [lambda e: e.activation(out=rr[0:1, :], in_=var[0:1, :],
                                                        func=AF.Ln, bias=eps11[0:1, 0:1], scale=1.0)],
                           sem="s", waits=[m_v3])
            m_s2 = em.step("s", [lambda e: e.activation(out=rr[0:1, :], in_=rr[0:1, :],
                                                        func=AF.Exp, scale=-0.5)],
                           sem="s", waits=[m_s1])
            m_v4 = em.step("v", [
                lambda e: e.tensor_copy(out=abf[0:1, :], in_=rr[0:1, :]),
                lambda e: e.scalar_tensor_tensor(out=bbf[0:1, :], in0=mu[0:1, :],
                                                 scalar=-1.0, in1=rr[0:1, :],
                                                 op0=ALU.mult, op1=ALU.mult),
                lambda e: e.tensor_scalar(out=rneg[0:1, :], in0=rr[0:1, :], scalar1=-1.0,
                                          scalar2=None, op0=ALU.mult),
            ], sem="v", waits=[m_s2, B.get("ab_free")])
            bA = pool.acquire()
            bB = pool.acquire()
            m_ab = em.step("pe", [
                (lambda b=bA: (lambda e: e.matmul(b["t"][:, 0:CH], ones1b[0:1, :],
                                                  abf[0:1, :], start=True, stop=True)))(),
                (lambda b=bB: (lambda e: e.matmul(b["t"][:, 0:CH], ones1b[0:1, :],
                                                  bbf[0:1, :], start=True, stop=True)))(),
            ], sem="pe", waits=[m_v4, bA["free"], bB["free"]])
            m_t1 = em.step("v", [
                (lambda k, b=bA: (lambda e: e.tensor_tensor(
                    out=x32c[:, k * CH:(k + 1) * CH], in0=x32c[:, k * CH:(k + 1) * CH],
                    in1=b["t"][:, 0:CH], op=ALU.mult)))(k)
                for k in range(KD)
            ], sem="v", waits=[m_ab, m_graw])
            pool.release(bA, m_t1)
            m_xn = em.step("v", [
                (lambda k, b=bB: (lambda e: e.tensor_tensor(
                    out=xn_slice(k), in0=x32c[:, k * CH:(k + 1) * CH],
                    in1=b["t"][:, 0:CH], op=ALU.add)))(k)
                for k in range(KD)
            ], sem="v", waits=[m_t1])
            pool.release(bB, m_xn)

            mk = {"x": m_x, "xn": m_xn}
            B["ab_free"] = m_ab
            if routing is None:
                B["free"][par] = [m_xn, m_v4]
                B["mu_free"] = m_v4
                mk["readers"] = [m_xn, m_v4]
                return mk

            R = routing
            ro = (par * NTT) * 8
            tb = 128 + c * 16
            m_tr = em.step("pe", [
                (lambda tt: (lambda e: e.matmul(
                    psS[:, tb + tt * 2:tb + tt * 2 + 1],
                    rneg[0:1, tt * 128:(tt + 1) * 128], ones11f[0:1, 0:1],
                    start=True, stop=True)))(tt)
                for tt in range(NTT)
            ] + [
                (lambda tt: (lambda e: e.matmul(
                    psS[:, tb + tt * 2 + 1:tb + tt * 2 + 2],
                    mu[0:1, tt * 128:(tt + 1) * 128], ones11f[0:1, 0:1],
                    start=True, stop=True)))(tt)
                for tt in range(NTT)
            ], sem="pe", waits=[m_v4])
            m_rmu = em.step("v", [lambda e: e.tensor_copy(
                out=R["rmu"][:, par * 2 * NTT:par * 2 * NTT + 2 * NTT],
                in_=psS[:, tb:tb + 2 * NTT])],
                sem="v", waits=[m_tr])
            rb = par * 2 * NTT
            m_l1 = em.step("v", [
                (lambda tt: (lambda e: e.scalar_tensor_tensor(
                    out=R["glog"][:, ro + tt * 8:ro + (tt + 1) * 8], in0=cs_sb[:, 0:E],
                    scalar=R["rmu"][:, rb + tt * 2 + 1:rb + tt * 2 + 2],
                    in1=psS[:, c * 32 + tt * 8:c * 32 + tt * 8 + 8],
                    op0=ALU.mult, op1=ALU.subtract)))(tt)
                for tt in range(NTT)
            ], sem="v", waits=[m_rmu, m_graw, m_cs2])
            em.ewait("v", m_l1)
            m_l2 = em.step("v", [
                (lambda tt: (lambda e: e.tensor_scalar(
                    out=R["glog"][:, ro + tt * 8:ro + (tt + 1) * 8],
                    in0=R["glog"][:, ro + tt * 8:ro + (tt + 1) * 8],
                    scalar1=R["rmu"][:, rb + tt * 2:rb + tt * 2 + 1], scalar2=None,
                    op0=ALU.mult)))(tt)
                for tt in range(NTT)
            ], sem="v", waits=[m_l1])
            em.ewait("v", m_l2)
            m_mx = em.step("v", [
                (lambda tt: (lambda e: e.max(out=R["vmax"][:, ro + tt * 8:ro + (tt + 1) * 8],
                                             in_=R["glog"][:, ro + tt * 8:ro + (tt + 1) * 8])))(tt)
                for tt in range(NTT)
            ], sem="v", waits=[m_l2])
            em.ewait("v", m_mx)
            m_sel = em.step("v", [
                (lambda tt: (lambda e: e.tensor_scalar(
                    out=R["negm"][:, par * NTT + tt:par * NTT + tt + 1],
                    in0=R["vmax"][:, ro + tt * 8:ro + tt * 8 + 1],
                    scalar1=-1.0, scalar2=None, op0=ALU.mult)))(tt)
                for tt in range(NTT)
            ] + [
                (lambda tt: (lambda e: e.tensor_scalar(
                    out=R["sel8"][:, ro + tt * 8:ro + (tt + 1) * 8],
                    in0=R["glog"][:, ro + tt * 8:ro + (tt + 1) * 8],
                    scalar1=R["vmax"][:, ro + tt * 8 + 1:ro + tt * 8 + 2], scalar2=None,
                    op0=ALU.is_ge)))(tt)
                for tt in range(NTT)
            ], sem="v", waits=[m_mx])
            m_exp = em.step("s", [
                (lambda tt: (lambda e: e.activation(
                    out=R["esb"][:, ro + tt * 8:ro + (tt + 1) * 8],
                    in_=R["glog"][:, ro + tt * 8:ro + (tt + 1) * 8],
                    func=AF.Exp, bias=R["negm"][:, par * NTT + tt:par * NTT + tt + 1],
                    scale=1.0)))(tt)
                for tt in range(NTT)
            ], sem="s", waits=[m_sel])
            m_sum = em.step("v", [
                (lambda tt: (lambda e: e.tensor_reduce(
                    out=R["ssum"][:, par * NTT + tt:par * NTT + tt + 1],
                    in_=R["esb"][:, ro + tt * 8:ro + (tt + 1) * 8],
                    op=ALU.add, axis=AX.X)))(tt)
                for tt in range(NTT)
            ], sem="v", waits=[m_exp])
            em.ewait("v", m_sum)
            m_rs = em.step("v", [
                lambda e: e.reciprocal(out=R["rsum"][:, par * NTT:(par + 1) * NTT],
                                       in_=R["ssum"][:, par * NTT:(par + 1) * NTT]),
            ], sem="v", waits=[m_sum])
            em.ewait("v", m_rs)
            m_ss = em.step("v", [
                (lambda tt: (lambda e: e.scalar_tensor_tensor(
                    out=ssel_slice(tt), in0=R["esb"][:, ro + tt * 8:ro + (tt + 1) * 8],
                    scalar=R["rsum"][:, par * NTT + tt:par * NTT + tt + 1],
                    in1=R["sel8"][:, ro + tt * 8:ro + (tt + 1) * 8],
                    op0=ALU.mult, op1=ALU.mult)))(tt)
                for tt in range(NTT)
            ], sem="v", waits=[m_rs])
            mk["ssel"] = m_ss
            B["free"][par] = [m_xn, m_graw, m_tr, m_v4]
            B["mu_free"] = m_tr
            mk["readers"] = [m_xn, m_graw, m_tr, m_v4, m_l1, m_rmu, m_ss]
            return mk

        def mk_ln_bufs(SE, pref, nbuf=2):
            mk2 = lambda n, sh, ty: [SE(nc.sbuf_tensor(f"{pref}_{n}{i}", sh, ty))
                                     for i in range(nbuf)]
            return {
                "x32c": mk2("x32c", [128, KD * CH], F32),
                "x2b": [SE(nc.sbuf_tensor(f"{pref}_x2b", [128, KD * 128], BF16))],
                "mu": [SE(nc.sbuf_tensor(f"{pref}_mu", [1, CH], F32))],
                "var": mk2("var", [1, CH], F32),
                "rr": [SE(nc.sbuf_tensor(f"{pref}_rr", [1, CH], F32))],

                "abf": [SE(nc.sbuf_tensor(f"{pref}_abf", [1, CH], BF16))],
                "bbf": [SE(nc.sbuf_tensor(f"{pref}_bbf", [1, CH], BF16))],
                "free": [[] for _ in range(nbuf)],
            }

        def emit_ffn(SRC, DST, tag, init_fence):
            srcr = SRC.rearrange("(k p) t -> p k t", p=128)
            dstr = DST.rearrange("(k p) t -> p k t", p=128)
            if init_fence:
                wunits["free"] = [list(init_fence)] * 3
            with contextlib.ExitStack() as FS:
                FE = FS.enter_context
                xn_bf = FE(nc.sbuf_tensor(f"{tag}_xn", [128, KD * NS], BF16))
                y32 = FE(nc.sbuf_tensor(f"{tag}_y", [128, KD * NS], BF16))
                for i in range(3):
                    wslots[i] = FE(nc.sbuf_tensor(f"{tag}_w{i}", [128, KD * 1024], BF16))
                sT_sb = FE(nc.sbuf_tensor(f"{tag}_sT", [8, NS], BF16))
                s_b = FE(nc.sbuf_tensor(f"{tag}_sb", [128, 2 * NS], BF16))
                ssel_bf = FE(nc.sbuf_tensor(f"{tag}_ssel", [128, NCH * NTT * 8], BF16))
                routing = {
                    "glog": FE(nc.sbuf_tensor(f"{tag}_glog", [128, 2 * NTT * 8], F32)),
                    "vmax": FE(nc.sbuf_tensor(f"{tag}_vmax", [128, 2 * NTT * 8], F32)),
                    "negm": FE(nc.sbuf_tensor(f"{tag}_negm", [128, 2 * NTT], F32)),
                    "esb": FE(nc.sbuf_tensor(f"{tag}_esb", [128, 2 * NTT * 8], F32)),
                    "ssum": FE(nc.sbuf_tensor(f"{tag}_ssum", [128, 2 * NTT], F32)),
                    "rsum": FE(nc.sbuf_tensor(f"{tag}_rsum", [128, 2 * NTT], F32)),
                    "rmu": FE(nc.sbuf_tensor(f"{tag}_rmu", [128, 2 * 2 * NTT], F32)),
                    "sel8": FE(nc.sbuf_tensor(f"{tag}_sel8", [128, 2 * NTT * 8], F32)),
                }
                lb = mk_ln_bufs(FE, tag)
                banks = BankPool(em, [ps0, ps1, ps2, ps3, psA, psB])
                prev_fence = list(init_fence) if init_fence else []
                chunk_free = [None] * NCH
                for st in range(NST):
                    tok0 = st * NS
                    dense_done = None
                    with contextlib.ExitStack() as S2:
                        SE = S2.enter_context
                        g = SE(nc.sbuf_tensor(f"{tag}{st}_g", [128, KD * CH], BF16))
                        dct = {}

                        def dense_chunk(c, ih, sl, wm, xn_mark):
                            slA, mA, slB, mB, slC, mC = sl
                            wA, wB, wC = wm
                            g_marks = []
                            for it in range(KD):
                                bU1 = banks.acquire()
                                mU1 = em.step("pe", [
                                    (lambda k, it=it, b=bU1, c=c, wA=wA: (lambda e: e.matmul(
                                        b["t"][:, 0:CH],
                                        wA[:, k * 1024 + it * 128:k * 1024 + (it + 1) * 128],
                                        xn_bf[:, k * NS + c * CH:k * NS + (c + 1) * CH],
                                        start=(k == 0), stop=(k == KD - 1))))(k)
                                    for k in range(KD)
                                ], sem="pe", waits=[(f"w{slA}", mA[1]), bU1["free"],
                                                    xn_mark])
                                mS = em.step("s", [
                                    (lambda it=it, b=bU1: (lambda e: e.activation(
                                        out=g[:, it * CH:(it + 1) * CH],
                                        in_=b["t"][:, 0:CH], func=AF.Silu)))()
                                ], sem="s", waits=[mU1])
                                banks.release(bU1, mS)
                                dct["lastU1"] = mU1
                                bU2 = banks.acquire()
                                mU2 = em.step("pe", [
                                    (lambda k, it=it, b=bU2, c=c, wB=wB: (lambda e: e.matmul(
                                        b["t"][:, 0:CH],
                                        wB[:, k * 1024 + it * 128:k * 1024 + (it + 1) * 128],
                                        xn_bf[:, k * NS + c * CH:k * NS + (c + 1) * CH],
                                        start=(k == 0), stop=(k == KD - 1))))(k)
                                    for k in range(KD)
                                ], sem="pe", waits=[(f"w{slB}", mB[1]), bU2["free"]])
                                mG = em.step("v", [
                                    (lambda it=it, b=bU2: (lambda e: e.tensor_tensor(
                                        out=g[:, it * CH:(it + 1) * CH],
                                        in0=g[:, it * CH:(it + 1) * CH],
                                        in1=b["t"][:, 0:CH], op=ALU.mult)))()
                                ], sem="v", waits=[mU2, mS])
                                banks.release(bU2, mG)
                                dct["lastU2"] = mU2
                                g_marks.append(mG)
                            for dd in range(KD):
                                bZ = banks.acquire()
                                mZ = em.step("pe", [
                                    (lambda it, dd=dd, b=bZ, wC=wC: (lambda e: e.matmul(
                                        b["t"][:, 0:CH],
                                        wC[:, it * 1024 + dd * 128:it * 1024 + (dd + 1) * 128],
                                        g[:, it * CH:(it + 1) * CH],
                                        start=(it == 0), stop=(it == KD - 1))))(it)
                                    for it in range(KD)
                                ], sem="pe", waits=[(f"w{slC}", mC[1]), bZ["free"]] + g_marks)
                                if ih == 0:
                                    ywait = prev_fence if st == 0 else [chunk_free[c]]
                                    mY = em.step("s", [
                                        (lambda dd=dd, b=bZ, c=c: (lambda e: e.activation(
                                            out=y32[:, dd * NS + c * CH:dd * NS + (c + 1) * CH],
                                            in_=b["t"][:, 0:CH], func=AF.Identity)))()
                                    ], sem="s", waits=[mZ] + ywait)
                                else:
                                    mY = em.step("v", [
                                        (lambda dd=dd, b=bZ, c=c: (lambda e: e.tensor_tensor(
                                            out=y32[:, dd * NS + c * CH:dd * NS + (c + 1) * CH],
                                            in0=y32[:, dd * NS + c * CH:dd * NS + (c + 1) * CH],
                                            in1=b["t"][:, 0:CH], op=ALU.add)))()
                                    ], sem="v", waits=[mZ])
                                banks.release(bZ, mY)
                                dct["lastZ"] = mZ

                        def dense_slabs(ih):
                            slA, mA = load_weight(
                                SW1.rearrange("(k p) i -> p k i", p=128)[:, :, ih * 1024:(ih + 1) * 1024])
                            slB, mB = load_weight(
                                SW2.rearrange("(k p) i -> p k i", p=128)[:, :, ih * 1024:(ih + 1) * 1024])
                            slC, mC = load_weight(
                                SW3[ih * 1024:(ih + 1) * 1024, :].rearrange("(k p) d -> p k d", p=128))
                            return ((slA, mA, slB, mB, slC, mC),
                                    (wslots[slA], wslots[slB], wslots[slC]))

                        sl0, wm0 = dense_slabs(0)
                        tr_prev = None
                        tr_marks = []
                        mks = []

                        def emit_trans(c):
                            nonlocal tr_prev
                            m_t = em.step("pe", [
                                (lambda tt, c=c: (lambda e: e.transpose(
                                    psT[:, tt * 128:(tt + 1) * 128],
                                    ssel_bf[:, (c * NTT + tt) * 8:(c * NTT + tt + 1) * 8],
                                    identb[:, :])))(tt)
                                for tt in range(NTT)
                            ], sem="pe", waits=[mks[c]["ssel"], tr_prev])
                            m_e = em.step("v", [
                                (lambda c=c: (lambda e: e.tensor_copy(
                                    out=sT_sb[0:8, c * CH:(c + 1) * CH],
                                    in_=psT[0:8, 0:CH])))()
                            ], sem="v", waits=[m_t])
                            tr_prev = m_e
                            tr_marks.append(m_e)

                        for c in range(NCH):
                            t0 = tok0 + c * CH
                            if st == 0:
                                pm = list(prev_fence) if c < 2 else []
                            else:
                                pm = [chunk_free[c]]
                            mk = emit_ln_chunk(
                                srcr[:, :, t0:t0 + CH],
                                (lambda k, c=c: xn_bf[:, k * NS + c * CH:k * NS + (c + 1) * CH]),
                                routing,
                                (lambda tt, c=c: ssel_bf[:, (c * NTT + tt) * 8:(c * NTT + tt + 1) * 8]),
                                lb, pm, banks, c)
                            mks.append(mk)
                            if c >= 1:
                                dense_chunk(c - 1, 0, sl0, wm0, mks[c - 1]["xn"])
                                emit_trans(c - 1)
                        dense_chunk(NCH - 1, 0, sl0, wm0, mks[NCH - 1]["xn"])
                        emit_trans(NCH - 1)
                        wunits["free"][sl0[0]] = [dct["lastU1"]]
                        wunits["free"][sl0[2]] = [dct["lastU2"]]
                        wunits["free"][sl0[4]] = [dct["lastZ"]]
                        sl1, wm1 = dense_slabs(1)
                        for c in range(NCH):
                            dense_chunk(c, 1, sl1, wm1, None)
                        wunits["free"][sl1[0]] = [dct["lastU1"]]
                        wunits["free"][sl1[2]] = [dct["lastU2"]]
                        wunits["free"][sl1[4]] = [dct["lastZ"]]
                        dense_done = dct["lastZ"]
                    sb_reader = {0: None, 1: None}
                    sb_mark = {}

                    def sb_build(ee):
                        par = ee % 2
                        marks = []
                        for c in range(NCH):
                            bS = banks.acquire()
                            mS1 = em.step("pe", [
                                (lambda c=c, ee=ee, b=bS: (lambda e: e.matmul(
                                    b["t"][:, 0:CH],
                                    sel_all[0:8, ee * 128:(ee + 1) * 128],
                                    sT_sb[0:8, c * CH:(c + 1) * CH],
                                    start=True, stop=True)))()
                            ], sem="pe", waits=[bS["free"]] + tr_marks)
                            mS2 = em.step("v", [
                                (lambda c=c, par=par, b=bS: (lambda e: e.tensor_copy(
                                    out=s_b[:, par * NS + c * CH:par * NS + (c + 1) * CH],
                                    in_=b["t"][:, 0:CH])))()
                            ], sem="v", waits=[mS1, sb_reader[par]])
                            banks.release(bS, mS2)
                            marks.append(mS2)
                        sb_mark[ee] = marks

                    sb_build(0)
                    last_y = None
                    y_done = [None] * NCH
                    with contextlib.ExitStack() as S3:
                        SE = S3.enter_context
                        u = SE(nc.sbuf_tensor(f"{tag}{st}_u", [128, KD * CH], BF16))
                        h = SE(nc.sbuf_tensor(f"{tag}{st}_h", [128, KD * CH], BF16))
                        x3s = SE(nc.sbuf_tensor(f"{tag}{st}_x3", [128, KD * CH], BF16))
                        p = SE(nc.sbuf_tensor(f"{tag}{st}_p", [128, KD * CH // 2], F32))
                        wms = {}

                        def load_expert(ee):
                            sl1, m1 = load_weight(W1[ee].rearrange("(k p) h -> p k h", p=128))
                            sl3, m3 = load_weight(W3[ee].rearrange("(k p) h -> p k h", p=128))
                            sl2, m2 = load_weight(W2[ee].rearrange("(k p) d -> p k d", p=128))
                            wms[ee] = (sl1, m1, sl3, m3, sl2, m2,
                                       wslots[sl1], wslots[sl3], wslots[sl2])

                        load_expert(0)

                        def emit_u(ee, c, extra):
                            par = ee % 2
                            return em.step("v", [
                                (lambda k, c=c, par=par: (lambda e: e.tensor_tensor(
                                    out=u[:, k * CH:(k + 1) * CH],
                                    in0=xn_bf[:, k * NS + c * CH:k * NS + (c + 1) * CH],
                                    in1=s_b[:, par * NS + c * CH:par * NS + (c + 1) * CH],
                                    op=ALU.mult)))(k)
                                for k in range(KD)
                            ], sem="v", waits=[sb_mark[ee][c]] + extra)

                        u_mark = emit_u(0, 0, [dense_done])
                        m1_last = [None]
                        for ee in range(E):
                            sl1, m1, sl3, m3, sl2, m2, wt1, wt3, wt2 = wms[ee]
                            par = ee % 2
                            for c in range(NCH):
                                m_u = u_mark
                                h_marks = []
                                for ht in range(KD):
                                    b = banks.acquire()
                                    mM = em.step("pe", [
                                        (lambda k, ht=ht, b=b, w=wt1: (lambda e: e.matmul(
                                            b["t"][:, 0:CH],
                                            w[:, k * 1024 + ht * 128:k * 1024 + (ht + 1) * 128],
                                            u[:, k * CH:(k + 1) * CH],
                                            start=(k == 0), stop=(k == KD - 1))))(k)
                                        for k in range(KD)
                                    ], sem="pe", waits=[(f"w{sl1}", m1[1]), m_u, b["free"]])
                                    mH = em.step("s", [
                                        (lambda ht=ht, b=b, ee=ee: (lambda e: e.activation(
                                            out=h[:, ht * CH:(ht + 1) * CH],
                                            in_=b["t"][:, 0:CH], func=AF.Silu,
                                            bias=b1s[:, ht * E + ee:ht * E + ee + 1],
                                            scale=1.0)))()
                                    ], sem="s", waits=[mM])
                                    banks.release(b, mH)
                                    h_marks.append(mH)
                                m1_done_c = em.mark("pe")
                                x3_marks = []
                                for ht in range(KD):
                                    b = banks.acquire()
                                    mM = em.step("pe", [
                                        (lambda k, ht=ht, b=b, w=wt3: (lambda e: e.matmul(
                                            b["t"][:, 0:CH],
                                            w[:, k * 1024 + ht * 128:k * 1024 + (ht + 1) * 128],
                                            u[:, k * CH:(k + 1) * CH],
                                            start=(k == 0), stop=(k == KD - 1))))(k)
                                        for k in range(KD)
                                    ], sem="pe", waits=[(f"w{sl3}", m3[1]), b["free"]])
                                    mX3 = em.step("s", [
                                        (lambda ht=ht, b=b, ee=ee: (lambda e: e.activation(
                                            out=x3s[:, ht * CH:(ht + 1) * CH],
                                            in_=b["t"][:, 0:CH], func=AF.Identity,
                                            bias=b3s[:, ht * E + ee:ht * E + ee + 1],
                                            scale=1.0)))()
                                    ], sem="s", waits=[mM])
                                    banks.release(b, mX3)
                                    x3_marks.append(mX3)
                                m3_done = em.mark("pe")
                                if c == NCH - 1:
                                    m1_last[0] = m1_done_c
                                if c == 0 and ee + 1 < E:
                                    sb_build(ee + 1)
                                if c + 1 < NCH:
                                    u_mark = emit_u(ee, c + 1, [m3_done])
                                elif ee + 1 < E:
                                    u_mark = emit_u(ee + 1, 0, [m3_done])
                                m_y = None
                                for wave in range(2):
                                    p_marks = []
                                    for dd in range(wave * 4, wave * 4 + 4):
                                        b = banks.acquire()
                                        mM = em.step("pe", [
                                            (lambda k, dd=dd, b=b, w=wt2: (lambda e: e.matmul(
                                                b["t"][:, 0:CH],
                                                w[:, k * 1024 + dd * 128:k * 1024 + (dd + 1) * 128],
                                                h[:, k * CH:(k + 1) * CH],
                                                start=(k == 0), stop=(k == KD - 1))))(k)
                                            for k in range(KD)
                                        ], sem="pe", waits=[(f"w{sl2}", m2[1]), b["free"]] + h_marks)
                                        mP = em.step("v", [
                                            (lambda dd=dd, b=b, ee=ee: (lambda e: e.scalar_tensor_tensor(
                                                out=p[:, (dd % 4) * CH:(dd % 4 + 1) * CH],
                                                in0=b["t"][:, 0:CH],
                                                scalar=b2s[:, dd * E + ee:dd * E + ee + 1],
                                                in1=x3s[:, dd * CH:(dd + 1) * CH],
                                                op0=ALU.add, op1=ALU.mult)))()
                                        ], sem="v", waits=[mM, m_y if wave else last_y] + x3_marks)
                                        banks.release(b, mP)
                                        p_marks.append(mP)
                                    em.ewait("v", p_marks[-1])
                                    m_pm = em.step("v", [
                                        (lambda dd, c=c, par=par: (lambda e: e.scalar_tensor_tensor(
                                            out=p[:, (dd % 4) * CH:(dd % 4 + 1) * CH],
                                            in0=s_b[:, par * NS + c * CH:par * NS + (c + 1) * CH],
                                            scalar=0.0,
                                            in1=p[:, (dd % 4) * CH:(dd % 4 + 1) * CH],
                                            op0=ALU.is_gt, op1=ALU.mult)))(dd)
                                        for dd in range(wave * 4, wave * 4 + 4)
                                    ], sem="v", waits=p_marks)
                                    em.ewait("v", m_pm)
                                    m_y = em.step("v", [
                                        (lambda dd, c=c: (lambda e: e.tensor_tensor(
                                            out=y32[:, dd * NS + c * CH:dd * NS + (c + 1) * CH],
                                            in0=y32[:, dd * NS + c * CH:dd * NS + (c + 1) * CH],
                                            in1=p[:, (dd % 4) * CH:(dd % 4 + 1) * CH],
                                            op=ALU.add)))(dd)
                                        for dd in range(wave * 4, wave * 4 + 4)
                                    ], sem="v", waits=[m_pm])
                                last_y = m_y
                                if ee == E - 1:
                                    y_done[c] = m_y
                                if c == NCH - 1:
                                    sb_reader[par] = m_pm
                            wunits["free"][sl1] = [m1_last[0]]
                            wunits["free"][sl3] = [m3_done]
                            wunits["free"][sl2] = [em.mark("pe")]
                            if ee + 1 < E:
                                load_expert(ee + 1)
                        exp_fence = em.fence(engs=("pe", "v", "s"))
                    st_mk = {0: None, 1: None}
                    m_o_marks = []
                    for c in range(NCH):
                        t0 = tok0 + c * CH
                        par = c % 2
                        xr = lb["x32c"][par]
                        lsem = "xla" if par == 0 else "xlb"
                        ssem = "sta" if par == 0 else "stb"
                        m_lx = em.step("sy", [
                            (lambda t0=t0, xr=xr: (lambda e: e.dma_start(
                                out=xr[:, :].rearrange("p (k n) -> p k n", k=KD),
                                in_=srcr[:, :, t0:t0 + CH])))()
                        ], sem=lsem, inc=16,
                            waits=[st_mk[par], y_done[c]] + lb["free"][par])
                        m_o = em.step("v", [
                            (lambda k, c=c, xr=xr: (lambda e: e.scalar_tensor_tensor(
                                out=xr[:, k * CH:(k + 1) * CH],
                                in0=y32[:, k * NS + c * CH:k * NS + (c + 1) * CH],
                                scalar=0.5,
                                in1=xr[:, k * CH:(k + 1) * CH],
                                op0=ALU.mult, op1=ALU.add)))(k)
                            for k in range(KD)
                        ], sem="v", waits=[m_lx, y_done[c]])
                        m_o_marks.append(m_o)
                        m_st = em.step("sy", [
                            (lambda t0=t0, xr=xr: (lambda e: e.dma_start(
                                out=dstr[:, :, t0:t0 + CH],
                                in_=xr[:, :].rearrange("p (k n) -> p k n", k=KD))))()
                        ], sem=ssem, inc=16, waits=[m_o])
                        st_mk[par] = m_st
                        lb["free"][par] = [m_st]
                    chunk_free = m_o_marks
                    prev_fence = em.fence(engs=("pe", "v", "s"),
                                          dma=("sta", "stb"))
            return em.fence(engs=("pe", "v", "s"), dma=("sta", "stb"))

        def emit_attn(SRC, DST, ffn_fence):
            srcr = SRC.rearrange("(k p) t -> p k t", p=128)
            dstr = DST.rearrange("(k p) t -> p k t", p=128)
            NQC = T // CH
            NJT = T // 128
            banks = BankPool(em, [ps0, ps1, ps2, ps3])
            with contextlib.ExitStack() as AS:
                AE = AS.enter_context
                keff = AE(nc.sbuf_tensor("a_keff", [128, KD * L], BF16))
                vM = AE(nc.sbuf_tensor("a_vM", [128, 2 * D], BF16))
                wdkvt = AE(nc.sbuf_tensor("a_wdkvt", [128, KD * L], BF16))
                wdqt = AE(nc.sbuf_tensor("a_wdqt", [128, KD * L], BF16))
                wuqt = AE(nc.sbuf_tensor("a_wuqt", [128, 2 * D], BF16))
                masks = AE(nc.sbuf_tensor("a_masks", [128, 4 * CH], BF16))
                JSC = contextlib.ExitStack()
                jm2 = JSC.enter_context(nc.sbuf_tensor("a_jm2", [128, CH], I32))
                m_mask = []
                for v in range(4):
                    mi = em.step("gp", [
                        (lambda v=v: (lambda e: e.iota(jm2[:, :], [[1, CH]],
                                                       base=-128 * v,
                                                       channel_multiplier=-1)))()
                    ], sem="gp", waits=[m_mask[-1] if m_mask else None] + ffn_fence)
                    mv = em.step("v", [
                        (lambda v=v: (lambda e: e.tensor_scalar(
                            out=masks[:, v * CH:(v + 1) * CH], in0=jm2[:, :],
                            scalar1=0.0, scalar2=None, op0=ALU.is_ge)))()
                    ], sem="v", waits=[mi] + ffn_fence)
                    m_mask.append(mv)
                JSC.close()
                m_aw = em.step("gp", [
                    lambda e: e.dma_start(out=wdkvt[:, :].rearrange("p (k n) -> p k n", k=KD),
                                          in_=WDKVT.rearrange("(k p) n -> p k n", p=128)),
                    lambda e: e.dma_start(out=wdqt[:, :].rearrange("p (k n) -> p k n", k=KD),
                                          in_=WDQT.rearrange("(k p) n -> p k n", p=128)),
                    lambda e: e.dma_start(out=wuqt[:, :].rearrange("p (k n) -> p k n", k=2),
                                          in_=WUQT.rearrange("(k p) n -> p k n", p=128)),
                ], sem="aw", inc=16, inc_each=True, waits=ffn_fence)
                with contextlib.ExitStack() as PS:
                    PEx = PS.enter_context
                    t1t = PEx(nc.sbuf_tensor("a_t1t", [128, KD * D], BF16))
                    wot = PEx(nc.sbuf_tensor("a_wot", [128, KD * D], BF16))
                    wdq = PEx(nc.sbuf_tensor("a_wdq", [128, 2 * D], BF16))
                    wuk = PEx(nc.sbuf_tensor("a_wuk", [128, KD * L], BF16))
                    wuv = PEx(nc.sbuf_tensor("a_wuv", [128, KD * L], BF16))
                    m_aw2 = em.step("gp", [
                        lambda e: e.dma_start(out=wot[:, :].rearrange("p (k n) -> p k n", k=KD),
                                              in_=WOT.rearrange("(k p) n -> p k n", p=128)),
                        lambda e: e.dma_start(out=wdq[:, :].rearrange("p (k n) -> p k n", k=2),
                                              in_=WDQ.rearrange("(k p) n -> p k n", p=128)),
                        lambda e: e.dma_start(out=wuk[:, :].rearrange("p (k n) -> p k n", k=KD),
                                              in_=WUK.rearrange("(k p) n -> p k n", p=128)),
                        lambda e: e.dma_start(out=wuv[:, :].rearrange("p (k n) -> p k n", k=KD),
                                              in_=WUV.rearrange("(k p) n -> p k n", p=128)),
                    ], sem="aw", inc=16, inc_each=True, waits=ffn_fence)
                    AWALL = ("aw", 112)
                    t1_marks = []
                    for ct in range(KD):
                        for half in range(2):
                            b = banks.acquire()
                            mm = em.step("pe", [
                                (lambda k, ct=ct, half=half, b=b: (lambda e: e.matmul(
                                    b["t"][:, 0:CH],
                                    wuqt[:, k * D + ct * 128:k * D + (ct + 1) * 128],
                                    wdq[:, k * D + half * CH:k * D + (half + 1) * CH],
                                    start=(k == 0), stop=(k == 1))))(k)
                                for k in range(2)
                            ], sem="pe", waits=[AWALL, b["free"]] + ffn_fence)
                            me = em.step("v", [
                                (lambda ct=ct, half=half, b=b: (lambda e: e.tensor_copy(
                                    out=t1t[:, ct * D + half * CH:ct * D + (half + 1) * CH],
                                    in_=b["t"][:, 0:CH])))()
                            ], sem="v", waits=[mm] + m_mask)
                            banks.release(b, me)
                            t1_marks.append(me)
                    keff_marks = []
                    for ct in range(KD):
                        b = banks.acquire()
                        mm = em.step("pe", [
                            (lambda k, ct=ct, b=b: (lambda e: e.matmul(
                                b["t"][:, 0:L],
                                t1t[:, k * D + ct * 128:k * D + (ct + 1) * 128],
                                wuk[:, k * L:(k + 1) * L],
                                start=(k == 0), stop=(k == KD - 1))))(k)
                            for k in range(KD)
                        ], sem="pe", waits=[b["free"]] + t1_marks)
                        me = em.step("v", [
                            (lambda ct=ct, b=b: (lambda e: e.tensor_copy(
                                out=keff[:, ct * L:(ct + 1) * L], in_=b["t"][:, 0:L])))()
                        ], sem="v", waits=[mm])
                        banks.release(b, me)
                        keff_marks.append(me)
                    vm_marks = []
                    for lt in range(2):
                        for half in range(2):
                            b = banks.acquire()
                            mm = em.step("pe", [
                                (lambda k, lt=lt, half=half, b=b: (lambda e: e.matmul(
                                    b["t"][:, 0:CH],
                                    wuv[:, k * L + lt * 128:k * L + (lt + 1) * 128],
                                    wot[:, k * D + half * CH:k * D + (half + 1) * CH],
                                    start=(k == 0), stop=(k == KD - 1))))(k)
                                for k in range(KD)
                            ], sem="pe", waits=[b["free"]])
                            me = em.step("v", [
                                (lambda lt=lt, half=half, b=b: (lambda e: e.tensor_copy(
                                    out=vM[:, lt * D + half * CH:lt * D + (half + 1) * CH],
                                    in_=b["t"][:, 0:CH])))()
                            ], sem="v", waits=[mm])
                            banks.release(b, me)
                            vm_marks.append(me)
                    prep_fence = em.fence(engs=("pe", "v"))
                xn = AE(nc.sbuf_tensor("a_xn", [128, KD * T], BF16))
                qf = AE(nc.sbuf_tensor("a_qf", [128, KD * T], BF16))
                ya = AE(nc.sbuf_tensor("a_ya", [128, KD * T], BF16))
                ckf = AE(nc.sbuf_tensor("a_ckf", [128, 2 * T], BF16))
                ckt = AE(nc.sbuf_tensor("a_ckt", [128, NJT * L], BF16))
                qlat = AE(nc.sbuf_tensor("a_qlat", [128, 2 * T], BF16))
                Et = AE(nc.sbuf_tensor("a_Et", [128, 4 * CH], BF16))
                ctxs = AE(nc.sbuf_tensor("a_ctxs", [128, 2 * CH], BF16))
                rbf = AE(nc.sbuf_tensor("a_rbf", [1, CH], BF16))
                rln = AE(nc.sbuf_tensor("a_rln", [1, CH], F32))
                rbs = AE(nc.sbuf_tensor("a_rbs", [64, CH], F32))
                lba = mk_ln_bufs(AE, "aln", nbuf=2)
                seq_fence = prep_fence
                st_mk = {0: None, 1: None}
                et_free = [None, None, None, None]
                score_free = [None, None, None]
                pending_epi = [None, None]
                EPI = {"prev_rln": None, "prev_rbe": None, "prev_ymm": None,
                       "prev_ya": None, "sum_done": None, "ctx_done": None,
                       "prev_r": None, "prev_rbmm": None}
                for sq in range(NSEQ):
                    q0 = sq * T
                    xn_marks = []
                    ck_marks = []
                    qf_marks = []
                    mks = []
                    with contextlib.ExitStack() as QS:
                        SE = QS.enter_context
                        qlow = SE(nc.sbuf_tensor(f"as{sq}_qlow", [128, 2 * CH], BF16))
                        qfl = {"mm": None}

                        def proj_chunk(c):
                            xnm = mks[c]["xn"]
                            for lh in range(2):
                                b = banks.acquire()
                                mm = em.step("pe", [
                                    (lambda k, lh=lh, c=c, b=b: (lambda e: e.matmul(
                                        b["t"][:, 0:CH],
                                        wdkvt[:, k * L + lh * 128:k * L + (lh + 1) * 128],
                                        xn[:, k * T + c * CH:k * T + (c + 1) * CH],
                                        start=(k == 0), stop=(k == KD - 1))))(k)
                                    for k in range(KD)
                                ], sem="pe", waits=[b["free"], AWALL, xnm])
                                me = em.step("v", [
                                    (lambda lh=lh, c=c, b=b: (lambda e: e.tensor_copy(
                                        out=ckf[:, lh * T + c * CH:lh * T + (c + 1) * CH],
                                        in_=b["t"][:, 0:CH])))()
                                ], sem="v", waits=[mm])
                                banks.release(b, me)
                                ck_marks.append(me)
                            for jt in range(4 * c, 4 * c + 4):
                                b = banks.acquire()
                                mm = em.step("pe", [
                                    (lambda k, jt=jt, b=b: (lambda e: e.matmul(
                                        b["t"][:, 0:L],
                                        xn[:, k * T + jt * 128:k * T + (jt + 1) * 128],
                                        wdkvt[:, k * L:(k + 1) * L],
                                        start=(k == 0), stop=(k == KD - 1))))(k)
                                    for k in range(KD)
                                ], sem="pe", waits=[b["free"], xnm])
                                me = em.step("v", [
                                    (lambda jt=jt, b=b: (lambda e: e.tensor_copy(
                                        out=ckt[:, jt * L:(jt + 1) * L], in_=b["t"][:, 0:L])))()
                                ], sem="v", waits=[mm])
                                banks.release(b, me)
                                ck_marks.append(me)
                            ql_marks = []
                            for lh in range(2):
                                b = banks.acquire()
                                mm = em.step("pe", [
                                    (lambda k, lh=lh, c=c, b=b: (lambda e: e.matmul(
                                        b["t"][:, 0:CH],
                                        wdqt[:, k * L + lh * 128:k * L + (lh + 1) * 128],
                                        xn[:, k * T + c * CH:k * T + (c + 1) * CH],
                                        start=(k == 0), stop=(k == KD - 1))))(k)
                                    for k in range(KD)
                                ], sem="pe", waits=[b["free"], AWALL, xnm])
                                me = em.step("v", [
                                    (lambda lh=lh, b=b: (lambda e: e.tensor_copy(
                                        out=qlow[:, lh * CH:(lh + 1) * CH],
                                        in_=b["t"][:, 0:CH])))()
                                ], sem="v", waits=[mm, qfl["mm"]])
                                banks.release(b, me)
                                ql_marks.append(me)
                            for ct in range(KD):
                                b = banks.acquire()
                                mm = em.step("pe", [
                                    (lambda k, ct=ct, b=b: (lambda e: e.matmul(
                                        b["t"][:, 0:CH],
                                        wuqt[:, k * D + ct * 128:k * D + (ct + 1) * 128],
                                        qlow[:, k * CH:(k + 1) * CH],
                                        start=(k == 0), stop=(k == 1))))(k)
                                    for k in range(2)
                                ], sem="pe", waits=[b["free"]] + ql_marks)
                                me = em.step("v", [
                                    (lambda ct=ct, c=c, b=b: (lambda e: e.tensor_copy(
                                        out=qf[:, ct * T + c * CH:ct * T + (c + 1) * CH],
                                        in_=b["t"][:, 0:CH])))()
                                ], sem="v", waits=[mm])
                                banks.release(b, me)
                                qf_marks.append(me)
                            qfl["mm"] = em.mark("pe")

                        for c in range(NQC):
                            t0 = q0 + c * CH
                            if sq == 0:
                                pm = list(seq_fence) if c < 2 else []
                            else:
                                pm = []
                            mk = emit_ln_chunk(
                                srcr[:, :, t0:t0 + CH],
                                (lambda k, c=c: xn[:, k * T + c * CH:k * T + (c + 1) * CH]),
                                None, None, lba, pm, banks, c)
                            mks.append(mk)
                            xn_marks.append(mk["xn"])
                            if c >= 1:
                                proj_chunk(c - 1)
                        proj_chunk(NQC - 1)
                    qpool = BankPool(em, [ps0, ps1, ps2])
                    ya_done = {}
                    for hh in range(NH):
                        ct_h = hh // 2
                        po_h = (hh % 2) * 64
                        qlat_marks = []
                        for lh in range(2):
                            for c in range(NQC):
                                b = qpool.acquire()
                                mm = em.step("pe", [
                                    (lambda lh=lh, c=c, b=b, ct_h=ct_h, po_h=po_h:
                                     (lambda e: e.matmul(
                                         b["t"][:, 0:CH],
                                         keff[po_h:po_h + 64,
                                              ct_h * L + lh * 128:ct_h * L + (lh + 1) * 128],
                                         qf[po_h:po_h + 64,
                                            ct_h * T + c * CH:ct_h * T + (c + 1) * CH],
                                         start=True, stop=True)))()
                                ], sem="pe", waits=[b["free"], score_free[0],
                                                    score_free[1], score_free[2]] +
                                    qf_marks + keff_marks)
                                me = em.step("v", [
                                    (lambda lh=lh, c=c, b=b: (lambda e: e.tensor_copy(
                                        out=qlat[:, lh * T + c * CH:lh * T + (c + 1) * CH],
                                        in_=b["t"][:, 0:CH])))()
                                ], sem="v", waits=[mm])
                                qpool.release(b, me)
                                qlat_marks.append(me)
                        for qc in range(NQC):
                            njt = 4 * (qc + 1)
                            last_e = {}
                            for j in range(njt + 2):
                                if j < njt:
                                    sb = (ps0, ps1, ps2)[j % 3]
                                    m_sc = em.step("pe", [
                                        (lambda lh, j=j, qc=qc, sb=sb: (lambda e: e.matmul(
                                            sb[:, 0:CH],
                                            ckf[:, lh * T + j * 128:lh * T + (j + 1) * 128],
                                            qlat[:, lh * T + qc * CH:lh * T + (qc + 1) * CH],
                                            start=(lh == 0), stop=(lh == 1))))(lh)
                                        for lh in range(2)
                                    ], sem="pe", waits=qlat_marks + ck_marks +
                                        [score_free[j % 3], et_free[j % 4]])
                                if j == 1 and pending_epi[0] is not None:
                                    pending_epi[0]()
                                    pending_epi[0] = None
                                if j == 3 and pending_epi[1] is not None:
                                    pending_epi[1]()
                                    pending_epi[1] = None
                                if j < njt:
                                    m_es = em.step("s", [
                                        (lambda j=j, sb=sb: (lambda e: e.activation(
                                            out=Et[:, (j % 4) * CH:(j % 4 + 1) * CH],
                                            in_=sb[:, 0:CH], func=AF.Exp, scale=0.125)))()
                                    ], sem="s", waits=[m_sc, et_free[j % 4]])
                                    score_free[j % 3] = m_es
                                    m_e = m_es
                                    if j >= 4 * qc:
                                        vv = j - 4 * qc
                                        m_e = em.step("v", [
                                            (lambda j=j, vv=vv: (lambda e: e.tensor_tensor(
                                                out=Et[:, (j % 4) * CH:(j % 4 + 1) * CH],
                                                in0=Et[:, (j % 4) * CH:(j % 4 + 1) * CH],
                                                in1=masks[:, vv * CH:(vv + 1) * CH],
                                                op=ALU.mult)))()
                                        ], sem="v", waits=[m_e] + m_mask)
                                    last_e[j] = m_e
                                jj = j - 2
                                if jj >= 0:
                                    m_cx = em.step("pe", [
                                        (lambda lh, jj=jj, njt=njt: (lambda e: e.matmul(
                                            (psA if lh == 0 else psB)[:, 0:CH],
                                            ckt[:, jj * L + lh * 128:jj * L + (lh + 1) * 128],
                                            Et[:, (jj % 4) * CH:(jj % 4 + 1) * CH],
                                            start=(jj == 0), stop=(jj == njt - 1))))(lh)
                                        for lh in range(2)
                                    ] + [
                                        (lambda jj=jj, njt=njt: (lambda e: e.matmul(
                                            psS[0:1, 0:CH], onesc[:, :],
                                            Et[:, (jj % 4) * CH:(jj % 4 + 1) * CH],
                                            start=(jj == 0), stop=(jj == njt - 1))))(jj)
                                    ], sem="pe", waits=[last_e[jj]] +
                                        ([EPI["ctx_done"], EPI["sum_done"]] if jj == 0 else []))
                                    et_free[jj % 4] = m_cx
                            pe_tail = em.mark("pe")

                            def epiA(pe_tail=pe_tail):
                                m_rl = em.step("s", [
                                    lambda e: e.activation(out=rln[0:1, :],
                                                           in_=psS[0:1, 0:CH],
                                                           func=AF.Ln, scale=1.0),
                                ], sem="s", waits=[pe_tail, EPI["prev_rln"]])
                                m_r = em.step("s", [
                                    lambda e: e.activation(out=rbf[0:1, :],
                                                           in_=rln[0:1, :],
                                                           func=AF.Exp, scale=-1.0),
                                ], sem="s", waits=[m_rl, EPI["prev_rbmm"]])
                                m_cxe = em.step("v", [
                                    (lambda lh: (lambda e: e.tensor_copy(
                                        out=ctxs[:, lh * CH:(lh + 1) * CH],
                                        in_=(psA if lh == 0 else psB)[:, 0:CH])))(lh)
                                    for lh in range(2)
                                ], sem="v", waits=[pe_tail, EPI["prev_ymm"]])
                                EPI.update(prev_rln=m_r, prev_r=m_r,
                                           sum_done=m_rl, ctx_done=m_cxe)

                            def epiB(hh=hh, qc=qc, ct_h=ct_h, po_h=po_h):
                                m_r = EPI["prev_r"]
                                m_cxe = EPI["ctx_done"]
                                m_rb = em.step("pe", [
                                    lambda e: e.matmul(ps3[64:128, 0:CH],
                                                       ones1b[0:1, 0:64],
                                                       rbf[0:1, :], start=True,
                                                       stop=True),
                                ], sem="pe", waits=[m_r, EPI["prev_rbe"],
                                                    EPI["prev_ya"]])
                                EPI["prev_rbmm"] = m_rb
                                m_rbe = em.step("v", [
                                    lambda e: e.tensor_copy(out=rbs[0:64, :],
                                                            in_=ps3[64:128, 0:CH]),
                                ], sem="v", waits=[m_rb, EPI["prev_ya"]])
                                m_y = em.step("pe", [
                                    (lambda lh, hh=hh: (lambda e: e.matmul(
                                        ps3[0:64, 0:CH],
                                        vM[:, lh * D + hh * 64:lh * D + (hh + 1) * 64],
                                        ctxs[:, lh * CH:(lh + 1) * CH],
                                        start=(lh == 0), stop=(lh == 1))))(lh)
                                    for lh in range(2)
                                ], sem="pe", waits=[m_cxe, m_rbe, EPI["prev_ya"]] + vm_marks)
                                m_ye = em.step("v", [
                                    (lambda hh=hh, qc=qc, ct_h=ct_h, po_h=po_h:
                                     (lambda e: e.tensor_tensor(
                                         out=ya[po_h:po_h + 64,
                                                ct_h * T + qc * CH:ct_h * T + (qc + 1) * CH],
                                         in0=ps3[0:64, 0:CH], in1=rbs[0:64, :],
                                         op=ALU.mult)))()
                                ], sem="v", waits=[m_y])
                                if hh == NH - 1:
                                    ya_done[qc] = m_ye
                                EPI.update(prev_rbe=m_rbe, prev_ymm=m_y,
                                           prev_ya=m_ye)
                            pending_epi[0] = epiA
                            pending_epi[1] = epiB
                    for pi in range(2):
                        if pending_epi[pi] is not None:
                            pending_epi[pi]()
                            pending_epi[pi] = None
                    banks.banks[0]["free"] = [score_free[0]]
                    banks.banks[1]["free"] = [score_free[1]]
                    banks.banks[2]["free"] = [score_free[2]]
                    banks.banks[3]["free"] = [EPI["prev_rbe"], EPI["prev_ya"]]
                    for c in range(NQC):
                        t0 = q0 + c * CH
                        par = c % 2
                        xr = lba["x32c"][par]
                        lsem = "xla" if par == 0 else "xlb"
                        ssem = "sta" if par == 0 else "stb"
                        m_lx = em.step("sy", [
                            (lambda t0=t0, xr=xr: (lambda e: e.dma_start(
                                out=xr[:, :].rearrange("p (k n) -> p k n", k=KD),
                                in_=srcr[:, :, t0:t0 + CH])))()
                        ], sem=lsem, inc=16,
                            waits=[st_mk[par], ya_done[c]] + lba["free"][par])
                        m_o = em.step("v", [
                            (lambda k, c=c, xr=xr: (lambda e: e.tensor_tensor(
                                out=xr[:, k * CH:(k + 1) * CH],
                                in0=ya[:, k * T + c * CH:k * T + (c + 1) * CH],
                                in1=xr[:, k * CH:(k + 1) * CH], op=ALU.add)))(k)
                            for k in range(KD)
                        ], sem="v", waits=[m_lx, ya_done[c]])
                        m_st = em.step("sy", [
                            (lambda t0=t0, xr=xr: (lambda e: e.dma_start(
                                out=dstr[:, :, t0:t0 + CH],
                                in_=xr[:, :].rearrange("p (k n) -> p k n", k=KD))))()
                        ], sem=ssem, inc=16, waits=[m_o])
                        st_mk[par] = m_st
                        lba["free"][par] = [m_st]
            return em.fence(engs=("pe", "v", "s"), dma=("sta", "stb"))

        f1 = emit_ffn(X0, XB1, "f1", None)
        a1 = emit_attn(XB1, XB2, f1)
        f2 = emit_ffn(XB2, OUT, "f2", a1)
        em.ewait("sy", em.mark("sta"))
        em.ewait("sy", em.mark("stb"))

        with nc.Block() as block:
            @block.tensor
            def _(t):
                for f in em.q["pe"]:
                    f(t)

            @block.vector
            def _(v):
                for f in em.q["v"]:
                    f(v)

            @block.scalar
            def _(s):
                for f in em.q["s"]:
                    f(s)

            @block.gpsimd
            def _(g):
                for f in em.q["gp"]:
                    f(g)

            @block.sync
            def _(sy):
                for f in em.q["sy"]:
                    f(sy)

    return nc


def _prep_weights(inp):
    c = np.ascontiguousarray
    w = {}
    w["W1"] = c(np.asarray(inp["L0_ffn_w1"]).transpose(0, 2, 1))
    w["W2"] = c(np.asarray(inp["L0_ffn_w2"]).transpose(0, 2, 1))
    w["W3"] = c(np.asarray(inp["L0_ffn_w3"]).transpose(0, 2, 1))
    w["SW1"] = c(np.asarray(inp["L0_ffn_sw1"]).T)
    w["SW2"] = c(np.asarray(inp["L0_ffn_sw2"]).T)
    w["SW3"] = c(np.asarray(inp["L0_ffn_sw3"]).T)
    for nm, src in (("B1", "L0_ffn_b1"), ("B2", "L0_ffn_b2"), ("B3", "L0_ffn_b3")):
        w[nm] = c(np.asarray(inp[src]).reshape(E, KD, 128).transpose(2, 1, 0))
    w["GW"] = c(np.asarray(inp["L0_ffn_gate_w"]).T)
    w["WDQ"] = c(np.asarray(inp["L0_attn_W_dq"]))
    w["WDQT"] = c(np.asarray(inp["L0_attn_W_dq"]).T)
    w["WUQT"] = c(np.asarray(inp["L0_attn_W_uq"]).T)
    w["WDKVT"] = c(np.asarray(inp["L0_attn_W_dkv"]).T)
    w["WUK"] = c(np.asarray(inp["L0_attn_W_uk"]))
    w["WUV"] = c(np.asarray(inp["L0_attn_W_uv"]))
    w["WOT"] = c(np.asarray(inp["L0_attn_W_o"]).T)
    return w


def kernel(**inputs):
    NSEQ, T = 4, 2048
    n_cores = 8
    x = np.asarray(inputs["L0_attn_x"])
    w = _prep_weights(inputs)
    nc = build_program(NSEQ=NSEQ, T=T)
    in_maps = []
    for ci in range(n_cores):
        xs = x[ci * NSEQ:(ci + 1) * NSEQ].reshape(NSEQ * T, D)
        m = {"X0": np.ascontiguousarray(xs.T)}
        m.update(w)
        in_maps.append(m)
    res = run_bass_kernel_spmd(nc, in_maps, core_ids=list(range(n_cores)))
    outs = []
    for ci in range(n_cores):
        o = np.asarray(res.results[ci]["OUT"])
        outs.append(o.T.reshape(NSEQ, T, D))
    return np.concatenate(outs, axis=0)


# version 14 (final)
# speedup vs baseline: 1.0742x; 1.0742x over previous
import contextlib
import numpy as np

import concourse.bass as bass
import concourse.mybir as mybir
from concourse.bass_utils import run_bass_kernel_spmd

F32 = mybir.dt.float32
BF16 = mybir.dt.bfloat16
I32 = mybir.dt.int32
AF = mybir.ActivationFunctionType
ALU = mybir.AluOpType
AX = mybir.AxisListType

D = 1024
KD = 8
E = 8
CH = 512
L = 256
NH = 16
II = 2048
LN_EPS = 1e-5


def _flat(waits):
    out = []
    for w in waits:
        if w is None:
            continue
        if isinstance(w, list):
            out.extend(x for x in w if x is not None)
        else:
            out.append(w)
    return out


class Em:
    ENGS = ("pe", "v", "s", "gp", "sy")

    def __init__(self, nc, ctx, sem_names):
        self.nc = nc
        self.q = {k: [] for k in self.ENGS}
        self.sem = {}
        self.cnt = {}
        self.waited = {k: {} for k in self.ENGS}
        for name in sem_names:
            self.sem[name] = ctx.enter_context(nc.semaphore(name))
            self.cnt[name] = 0

    def mark(self, sem):
        return (sem, self.cnt[sem])

    def ewait(self, eng, mark):
        if mark is None:
            return
        sem, val = mark
        if val <= 0:
            return
        assert val <= self.cnt[sem], f"wait on future: {eng} {sem} {val}>{self.cnt[sem]}"
        if self.waited[eng].get(sem, 0) >= val:
            return
        self.waited[eng][sem] = val
        s = self.sem[sem]
        self.q[eng].append(lambda e, s=s, val=val: e.wait_ge(s, val))

    def step(self, eng, fns, sem=None, inc=1, waits=(), inc_each=False):
        for m in _flat(waits):
            self.ewait(eng, m)
        if not isinstance(fns, (list, tuple)):
            fns = [fns]
        assert fns
        for i, fn in enumerate(fns):
            if sem is not None and (inc_each or i == len(fns) - 1):
                s = self.sem[sem]
                self.q[eng].append(
                    lambda e, fn=fn, s=s, inc=inc: fn(e).then_inc(s, inc))
                self.cnt[sem] += inc
            else:
                self.q[eng].append(lambda e, fn=fn: fn(e))
        if sem is not None:
            return (sem, self.cnt[sem])
        return None

    def fence(self, engs=("pe", "v", "s"), dma=()):
        return [self.mark(k) for k in engs] + [self.mark(d) for d in dma]


class BankPool:

    def __init__(self, em, tensors):
        self.banks = [{"t": t, "free": None} for t in tensors]
        self.i = 0

    def acquire(self):
        b = self.banks[self.i % len(self.banks)]
        self.i += 1
        return b

    @staticmethod
    def release(b, mark):
        b["free"] = mark


def build_program(NSEQ=4, T=2048, NS=2048):
    TT = NSEQ * T
    NS = min(NS, TT)
    NST = TT // NS
    NCH = NS // CH
    NTT = CH // 128

    nc = bass.Bass("TRN2", target_bir_lowering=False, debug=False)
    dtn = lambda n, sh, ty, kind: nc.dram_tensor(n, sh, ty, kind=kind).ap()
    X0 = dtn("X0", [D, TT], F32, "ExternalInput")
    OUT = dtn("OUT", [D, TT], F32, "ExternalOutput")
    XB1 = dtn("XB1", [D, TT], F32, "Internal")
    XB2 = dtn("XB2", [D, TT], F32, "Internal")
    W1 = dtn("W1", [E, D, D], F32, "ExternalInput")
    W2 = dtn("W2", [E, D, D], F32, "ExternalInput")
    W3 = dtn("W3", [E, D, D], F32, "ExternalInput")
    SW1 = dtn("SW1", [D, II], F32, "ExternalInput")
    SW2 = dtn("SW2", [D, II], F32, "ExternalInput")
    SW3 = dtn("SW3", [II, D], F32, "ExternalInput")
    B1 = dtn("B1", [128, KD, E], F32, "ExternalInput")
    B2 = dtn("B2", [128, KD, E], F32, "ExternalInput")
    B3 = dtn("B3", [128, KD, E], F32, "ExternalInput")
    GW = dtn("GW", [D, E], F32, "ExternalInput")
    WDQ = dtn("WDQ", [L, D], F32, "ExternalInput")
    WDQT = dtn("WDQT", [D, L], F32, "ExternalInput")
    WUQT = dtn("WUQT", [L, D], F32, "ExternalInput")
    WDKVT = dtn("WDKVT", [D, L], F32, "ExternalInput")
    WUK = dtn("WUK", [D, L], F32, "ExternalInput")
    WUV = dtn("WUV", [D, L], F32, "ExternalInput")
    WOT = dtn("WOT", [D, D], F32, "ExternalInput")

    sems = ["pe", "v", "s", "gp", "w0", "w1", "w2", "xl", "xla", "xlb",
            "sta", "stb", "ld", "aw"]

    with contextlib.ExitStack() as CTX:
        EC = CTX.enter_context
        em = Em(nc, CTX, sems)

        ones_f = EC(nc.sbuf_tensor("ones_f", [128, 1], F32))
        ones_b = EC(nc.sbuf_tensor("ones_b", [128, 1], BF16))
        ones1b = EC(nc.sbuf_tensor("ones1b", [1, 128], BF16))
        ones11f = EC(nc.sbuf_tensor("ones11f", [1, 1], F32))
        ones1f = EC(nc.sbuf_tensor("ones1f", [1, 128], F32))
        eps11 = EC(nc.sbuf_tensor("eps11", [1, 1], F32))
        onesc = EC(nc.sbuf_tensor("onesc", [128, 1], BF16))
        identb = EC(nc.sbuf_tensor("identb", [128, 128], BF16))
        identf = EC(nc.sbuf_tensor("identf", [128, 128], F32))
        sel_all = EC(nc.sbuf_tensor("sel_all", [8, E * 128], BF16))
        b1s = EC(nc.sbuf_tensor("b1s", [128, KD * E], F32))
        b2s = EC(nc.sbuf_tensor("b2s", [128, KD * E], F32))
        b3s = EC(nc.sbuf_tensor("b3s", [128, KD * E], F32))
        gw_sb = EC(nc.sbuf_tensor("gw_sb", [128, KD * E], F32))
        cs_sb = EC(nc.sbuf_tensor("cs_sb", [128, E], F32))

        psA = EC(nc.psum_tensor("psA", [128, 512], F32))
        psB = EC(nc.psum_tensor("psB", [128, 512], F32))
        ps0 = EC(nc.psum_tensor("ps0", [128, 512], F32))
        ps1 = EC(nc.psum_tensor("ps1", [128, 512], F32))
        ps2 = EC(nc.psum_tensor("ps2", [128, 512], F32))
        ps3 = EC(nc.psum_tensor("ps3", [128, 512], F32))
        psS = EC(nc.psum_tensor("psS", [128, 512], F32))
        psT = EC(nc.psum_tensor("psT", [8, 512], BF16))

        CSC = contextlib.ExitStack()
        jmp = CSC.enter_context(nc.sbuf_tensor("jmp", [128, 128], I32))
        selk = CSC.enter_context(nc.sbuf_tensor("selk", [8, E * 128], I32))
        m_gp = em.step("gp", [
            lambda e: e.memset(ones_f[:, :], 1.0 / D),
            lambda e: e.memset(ones_b[:, :], 1.0 / D),
            lambda e: e.memset(ones1b[:, :], 1.0),
            lambda e: e.memset(ones11f[:, :], 1.0),
            lambda e: e.memset(eps11[:, :], LN_EPS),
            lambda e: e.memset(ones1f[:, :], 1.0),
            lambda e: e.memset(onesc[:, :], 1.0),
            lambda e: e.iota(jmp[:, :], [[1, 128]], base=0, channel_multiplier=-1),
            lambda e: e.iota(selk[:, :], [[0, E * 128]], base=0,
                             channel_multiplier=1),
        ], sem="gp")
        m_ld = em.step("sy", [
            lambda e: e.dma_start(out=b1s[:, :].rearrange("p (k e) -> p k e", k=KD), in_=B1),
            lambda e: e.dma_start(out=b2s[:, :].rearrange("p (k e) -> p k e", k=KD), in_=B2),
            lambda e: e.dma_start(out=b3s[:, :].rearrange("p (k e) -> p k e", k=KD), in_=B3),
            lambda e: e.dma_start(out=gw_sb[:, :].rearrange("p (k e) -> p k e", k=KD),
                                  in_=GW.rearrange("(k p) e -> p k e", p=128)),
        ], sem="ld", inc=16, inc_each=True)
        m_cb = em.step("v", [
            lambda e: e.tensor_scalar(out=identb[:, :], in0=jmp[:, :], scalar1=0.0,
                                      scalar2=None, op0=ALU.is_equal),
        ], sem="v", waits=[m_gp])
        m_consts = em.step("v", [
            lambda e: e.tensor_copy(out=identf[:, :], in_=identb[:, :]),
        ] + [
            (lambda ee: (lambda e: e.tensor_scalar(
                out=sel_all[0:8, ee * 128:(ee + 1) * 128],
                in0=selk[0:8, ee * 128:(ee + 1) * 128],
                scalar1=float(ee), scalar2=None, op0=ALU.is_equal)))(ee)
            for ee in range(E)
        ], sem="v", waits=[m_gp, m_cb])
        CSC.close()
        m_cs0 = em.step("pe", [
            (lambda k: (lambda e: e.matmul(psS[0:1, 0:E],
                                           ones_f[:, :], gw_sb[:, k * E:(k + 1) * E],
                                           start=(k == 0), stop=(k == KD - 1))))(k)
            for k in range(KD)
        ], sem="pe", waits=[m_ld, m_gp, m_consts])
        m_cs1 = em.step("v", [
            lambda e: e.tensor_scalar(out=cs_sb[0:1, :], in0=psS[0:1, 0:E],
                                      scalar1=float(D), scalar2=None, op0=ALU.mult),
        ], sem="v", waits=[m_cs0])
        m_cs1b = em.step("pe", [
            lambda e: e.matmul(psS[:, 0:E], ones1f[0:1, :], cs_sb[0:1, :],
                               start=True, stop=True),
        ], sem="pe", waits=[m_cs1])
        m_cs2 = em.step("v", [
            lambda e: e.tensor_copy(out=cs_sb[:, :], in_=psS[:, 0:E]),
        ], sem="v", waits=[m_cs1b])

        wunits = {"i": 0, "free": [None, None, None]}

        def load_weight(src_ap, extra_waits=()):
            slot = wunits["i"] % 3
            wunits["i"] += 1
            dst = wslots[slot]
            m = em.step("gp", [
                lambda e: e.dma_start(out=dst[:, :].rearrange(
                    "p (k n) -> p k n", k=KD), in_=src_ap),
            ], sem=f"w{slot}", inc=16,
                waits=_flat([wunits["free"][slot]] + list(extra_waits)))
            return slot, m

        wslots = [None, None, None]

        def emit_ln_chunk(src_slice, xn_slice, routing, ssel_slice, B, prev_marks,
                          pool, c):
            par = c % len(B["x32c"])
            x32c, x2b = B["x32c"][par], B["x2b"][0]
            mu, var, rr = B["mu"][0], B["var"][par], B["rr"][0]
            rneg = var
            abf, bbf = B["abf"][0], B["bbf"][0]
            m_x = em.step("sy", [lambda e: e.dma_start(out=x32c[:, :].rearrange(
                "p (k n) -> p k n", k=KD), in_=src_slice)],
                sem="xla" if par == 0 else "xlb", inc=16,
                waits=prev_marks + B["free"][par])
            bmu = pool.acquire()
            bex = pool.acquire()
            m_ex2 = B.get("ex2_last")
            for hh in range(4):
                m_sqh = em.step("s", [
                    (lambda k, hh=hh: (lambda e: e.activation(
                        out=x2b[:, k * 128:(k + 1) * 128],
                        in_=x32c[:, k * CH + hh * 128:k * CH + hh * 128 + 128],
                        func=AF.Square)))(k)
                    for k in range(KD)
                ], sem="s", waits=[m_x, m_ex2])
                m_ex2 = em.step("pe", [
                    (lambda k, hh=hh, b=bex: (lambda e: e.matmul(
                        b["t"][0:1, hh * 128:(hh + 1) * 128], ones_b[:, :],
                        x2b[:, k * 128:(k + 1) * 128],
                        start=(k == 0), stop=(k == KD - 1))))(k)
                    for k in range(KD)
                ], sem="pe", waits=[m_sqh, m_gp, bex["free"] if hh == 0 else None])
            B["ex2_last"] = m_ex2
            B["mu_free"] = B.get("mu_free")
            m_mu = em.step("pe", [
                (lambda k, b=bmu: (lambda e: e.matmul(
                    b["t"][0:1, 0:CH], ones_f[:, :],
                    x32c[:, k * CH:(k + 1) * CH],
                    start=(k == 0), stop=(k == KD - 1))))(k)
                for k in range(KD)
            ], sem="pe", waits=[m_x, m_gp, bmu["free"]])
            m_graw = None
            if routing is not None:
                m_graw = em.step("pe", [
                    (lambda k, tt: (lambda e: e.matmul(
                        psS[:, c * 32 + tt * 8:c * 32 + tt * 8 + 8],
                        x32c[:, k * CH + tt * 128:k * CH + tt * 128 + 128],
                        gw_sb[:, k * E:(k + 1) * E],
                        start=(k == 0), stop=(k == KD - 1))))(k, tt)
                    for tt in range(NTT) for k in range(KD)
                ], sem="pe", waits=[m_x, m_ld, m_cs2])
            m_v1 = em.step("v", [(lambda b=bmu: (lambda e: e.tensor_copy(
                out=mu[0:1, :], in_=b["t"][0:1, 0:CH])))()],
                sem="v", waits=[m_mu, B.get("mu_free")])
            pool.release(bmu, m_v1)
            m_v2 = em.step("v", [lambda e: e.tensor_tensor(out=var[0:1, :], in0=mu[0:1, :],
                                                           in1=mu[0:1, :], op=ALU.mult)],
                           sem="v", waits=[m_v1])
            m_v3 = em.step("v", [(lambda b=bex: (lambda e: e.tensor_tensor(
                out=var[0:1, :], in0=b["t"][0:1, 0:CH],
                in1=var[0:1, :], op=ALU.subtract)))()],
                sem="v", waits=[m_v2, m_ex2])
            pool.release(bex, m_v3)
            m_s1 = em.step("s", [lambda e: e.activation(out=rr[0:1, :], in_=var[0:1, :],
                                                        func=AF.Ln, bias=eps11[0:1, 0:1], scale=1.0)],
                           sem="s", waits=[m_v3])
            m_s2 = em.step("s", [lambda e: e.activation(out=rr[0:1, :], in_=rr[0:1, :],
                                                        func=AF.Exp, scale=-0.5)],
                           sem="s", waits=[m_s1])
            m_v4 = em.step("v", [
                lambda e: e.tensor_copy(out=abf[0:1, :], in_=rr[0:1, :]),
                lambda e: e.scalar_tensor_tensor(out=bbf[0:1, :], in0=mu[0:1, :],
                                                 scalar=-1.0, in1=rr[0:1, :],
                                                 op0=ALU.mult, op1=ALU.mult),
                lambda e: e.tensor_scalar(out=rneg[0:1, :], in0=rr[0:1, :], scalar1=-1.0,
                                          scalar2=None, op0=ALU.mult),
            ], sem="v", waits=[m_s2, B.get("ab_free")])
            bA = pool.acquire()
            bB = pool.acquire()
            m_ab = em.step("pe", [
                (lambda b=bA: (lambda e: e.matmul(b["t"][:, 0:CH], ones1b[0:1, :],
                                                  abf[0:1, :], start=True, stop=True)))(),
                (lambda b=bB: (lambda e: e.matmul(b["t"][:, 0:CH], ones1b[0:1, :],
                                                  bbf[0:1, :], start=True, stop=True)))(),
            ], sem="pe", waits=[m_v4, bA["free"], bB["free"]])
            m_t1 = em.step("v", [
                (lambda k, b=bA: (lambda e: e.tensor_tensor(
                    out=x32c[:, k * CH:(k + 1) * CH], in0=x32c[:, k * CH:(k + 1) * CH],
                    in1=b["t"][:, 0:CH], op=ALU.mult)))(k)
                for k in range(KD)
            ], sem="v", waits=[m_ab, m_graw])
            pool.release(bA, m_t1)
            m_xn = em.step("v", [
                (lambda k, b=bB: (lambda e: e.tensor_tensor(
                    out=xn_slice(k), in0=x32c[:, k * CH:(k + 1) * CH],
                    in1=b["t"][:, 0:CH], op=ALU.add)))(k)
                for k in range(KD)
            ], sem="v", waits=[m_t1])
            pool.release(bB, m_xn)

            mk = {"x": m_x, "xn": m_xn}
            B["ab_free"] = m_ab
            if routing is None:
                B["free"][par] = [m_xn, m_v4]
                B["mu_free"] = m_v4
                mk["readers"] = [m_xn, m_v4]
                return mk

            R = routing
            ro = (par * NTT) * 8
            tb = 128 + c * 16
            m_tr = em.step("pe", [
                (lambda tt: (lambda e: e.matmul(
                    psS[:, tb + tt * 2:tb + tt * 2 + 1],
                    rneg[0:1, tt * 128:(tt + 1) * 128], ones11f[0:1, 0:1],
                    start=True, stop=True)))(tt)
                for tt in range(NTT)
            ] + [
                (lambda tt: (lambda e: e.matmul(
                    psS[:, tb + tt * 2 + 1:tb + tt * 2 + 2],
                    mu[0:1, tt * 128:(tt + 1) * 128], ones11f[0:1, 0:1],
                    start=True, stop=True)))(tt)
                for tt in range(NTT)
            ], sem="pe", waits=[m_v4])
            m_rmu = em.step("v", [lambda e: e.tensor_copy(
                out=R["rmu"][:, par * 2 * NTT:par * 2 * NTT + 2 * NTT],
                in_=psS[:, tb:tb + 2 * NTT])],
                sem="v", waits=[m_tr])
            rb = par * 2 * NTT
            m_l1 = em.step("v", [
                (lambda tt: (lambda e: e.scalar_tensor_tensor(
                    out=R["glog"][:, ro + tt * 8:ro + (tt + 1) * 8], in0=cs_sb[:, 0:E],
                    scalar=R["rmu"][:, rb + tt * 2 + 1:rb + tt * 2 + 2],
                    in1=psS[:, c * 32 + tt * 8:c * 32 + tt * 8 + 8],
                    op0=ALU.mult, op1=ALU.subtract)))(tt)
                for tt in range(NTT)
            ], sem="v", waits=[m_rmu, m_graw, m_cs2])
            em.ewait("v", m_l1)
            m_l2 = em.step("v", [
                (lambda tt: (lambda e: e.tensor_scalar(
                    out=R["glog"][:, ro + tt * 8:ro + (tt + 1) * 8],
                    in0=R["glog"][:, ro + tt * 8:ro + (tt + 1) * 8],
                    scalar1=R["rmu"][:, rb + tt * 2:rb + tt * 2 + 1], scalar2=None,
                    op0=ALU.mult)))(tt)
                for tt in range(NTT)
            ], sem="v", waits=[m_l1])
            em.ewait("v", m_l2)
            m_mx = em.step("v", [
                (lambda tt: (lambda e: e.max(out=R["vmax"][:, ro + tt * 8:ro + (tt + 1) * 8],
                                             in_=R["glog"][:, ro + tt * 8:ro + (tt + 1) * 8])))(tt)
                for tt in range(NTT)
            ], sem="v", waits=[m_l2])
            em.ewait("v", m_mx)
            m_sel = em.step("v", [
                (lambda tt: (lambda e: e.tensor_scalar(
                    out=R["negm"][:, par * NTT + tt:par * NTT + tt + 1],
                    in0=R["vmax"][:, ro + tt * 8:ro + tt * 8 + 1],
                    scalar1=-1.0, scalar2=None, op0=ALU.mult)))(tt)
                for tt in range(NTT)
            ] + [
                (lambda tt: (lambda e: e.tensor_scalar(
                    out=R["sel8"][:, ro + tt * 8:ro + (tt + 1) * 8],
                    in0=R["glog"][:, ro + tt * 8:ro + (tt + 1) * 8],
                    scalar1=R["vmax"][:, ro + tt * 8 + 1:ro + tt * 8 + 2], scalar2=None,
                    op0=ALU.is_ge)))(tt)
                for tt in range(NTT)
            ], sem="v", waits=[m_mx])
            m_exp = em.step("s", [
                (lambda tt: (lambda e: e.activation(
                    out=R["esb"][:, ro + tt * 8:ro + (tt + 1) * 8],
                    in_=R["glog"][:, ro + tt * 8:ro + (tt + 1) * 8],
                    func=AF.Exp, bias=R["negm"][:, par * NTT + tt:par * NTT + tt + 1],
                    scale=1.0)))(tt)
                for tt in range(NTT)
            ], sem="s", waits=[m_sel])
            m_sum = em.step("v", [
                (lambda tt: (lambda e: e.tensor_reduce(
                    out=R["ssum"][:, par * NTT + tt:par * NTT + tt + 1],
                    in_=R["esb"][:, ro + tt * 8:ro + (tt + 1) * 8],
                    op=ALU.add, axis=AX.X)))(tt)
                for tt in range(NTT)
            ], sem="v", waits=[m_exp])
            em.ewait("v", m_sum)
            m_rs = em.step("v", [
                lambda e: e.reciprocal(out=R["rsum"][:, par * NTT:(par + 1) * NTT],
                                       in_=R["ssum"][:, par * NTT:(par + 1) * NTT]),
            ], sem="v", waits=[m_sum])
            em.ewait("v", m_rs)
            m_ss = em.step("v", [
                (lambda tt: (lambda e: e.scalar_tensor_tensor(
                    out=ssel_slice(tt), in0=R["esb"][:, ro + tt * 8:ro + (tt + 1) * 8],
                    scalar=R["rsum"][:, par * NTT + tt:par * NTT + tt + 1],
                    in1=R["sel8"][:, ro + tt * 8:ro + (tt + 1) * 8],
                    op0=ALU.mult, op1=ALU.mult)))(tt)
                for tt in range(NTT)
            ], sem="v", waits=[m_rs])
            mk["ssel"] = m_ss
            B["free"][par] = [m_xn, m_graw, m_tr, m_v4]
            B["mu_free"] = m_tr
            mk["readers"] = [m_xn, m_graw, m_tr, m_v4, m_l1, m_rmu, m_ss]
            return mk

        def mk_ln_bufs(SE, pref, nbuf=2):
            mk2 = lambda n, sh, ty: [SE(nc.sbuf_tensor(f"{pref}_{n}{i}", sh, ty))
                                     for i in range(nbuf)]
            return {
                "x32c": mk2("x32c", [128, KD * CH], F32),
                "x2b": [SE(nc.sbuf_tensor(f"{pref}_x2b", [128, KD * 128], BF16))],
                "mu": [SE(nc.sbuf_tensor(f"{pref}_mu", [1, CH], F32))],
                "var": mk2("var", [1, CH], F32),
                "rr": [SE(nc.sbuf_tensor(f"{pref}_rr", [1, CH], F32))],

                "abf": [SE(nc.sbuf_tensor(f"{pref}_abf", [1, CH], BF16))],
                "bbf": [SE(nc.sbuf_tensor(f"{pref}_bbf", [1, CH], BF16))],
                "free": [[] for _ in range(nbuf)],
            }

        def emit_ffn(SRC, DST, tag, init_fence):
            srcr = SRC.rearrange("(k p) t -> p k t", p=128)
            dstr = DST.rearrange("(k p) t -> p k t", p=128)
            if init_fence:
                wunits["free"] = [list(init_fence)] * 3
            with contextlib.ExitStack() as FS:
                FE = FS.enter_context
                xn_bf = FE(nc.sbuf_tensor(f"{tag}_xn", [128, KD * NS], BF16))
                y32 = FE(nc.sbuf_tensor(f"{tag}_y", [128, KD * NS], BF16))
                for i in range(3):
                    wslots[i] = FE(nc.sbuf_tensor(f"{tag}_w{i}", [128, KD * 1024], BF16))
                sT_sb = FE(nc.sbuf_tensor(f"{tag}_sT", [8, NS], BF16))
                s_b = FE(nc.sbuf_tensor(f"{tag}_sb", [128, 2 * NS], BF16))
                ssel_bf = FE(nc.sbuf_tensor(f"{tag}_ssel", [128, NCH * NTT * 8], BF16))
                routing = {
                    "glog": FE(nc.sbuf_tensor(f"{tag}_glog", [128, 2 * NTT * 8], F32)),
                    "vmax": FE(nc.sbuf_tensor(f"{tag}_vmax", [128, 2 * NTT * 8], F32)),
                    "negm": FE(nc.sbuf_tensor(f"{tag}_negm", [128, 2 * NTT], F32)),
                    "esb": FE(nc.sbuf_tensor(f"{tag}_esb", [128, 2 * NTT * 8], F32)),
                    "ssum": FE(nc.sbuf_tensor(f"{tag}_ssum", [128, 2 * NTT], F32)),
                    "rsum": FE(nc.sbuf_tensor(f"{tag}_rsum", [128, 2 * NTT], F32)),
                    "rmu": FE(nc.sbuf_tensor(f"{tag}_rmu", [128, 2 * 2 * NTT], F32)),
                    "sel8": FE(nc.sbuf_tensor(f"{tag}_sel8", [128, 2 * NTT * 8], F32)),
                }
                lb = mk_ln_bufs(FE, tag)
                banks = BankPool(em, [ps0, ps1, ps2, ps3, psA, psB])
                prev_fence = list(init_fence) if init_fence else []
                chunk_free = [None] * NCH
                for st in range(NST):
                    tok0 = st * NS
                    dense_done = None
                    with contextlib.ExitStack() as S2:
                        SE = S2.enter_context
                        g = SE(nc.sbuf_tensor(f"{tag}{st}_g", [128, KD * CH], BF16))
                        dct = {}

                        def dense_chunk(c, ih, sl, wm, xn_mark):
                            slA, mA, slB, mB, slC, mC = sl
                            wA, wB, wC = wm
                            g_marks = []
                            for it in range(KD):
                                bU1 = banks.acquire()
                                mU1 = em.step("pe", [
                                    (lambda k, it=it, b=bU1, c=c, wA=wA: (lambda e: e.matmul(
                                        b["t"][:, 0:CH],
                                        wA[:, k * 1024 + it * 128:k * 1024 + (it + 1) * 128],
                                        xn_bf[:, k * NS + c * CH:k * NS + (c + 1) * CH],
                                        start=(k == 0), stop=(k == KD - 1))))(k)
                                    for k in range(KD)
                                ], sem="pe", waits=[(f"w{slA}", mA[1]), bU1["free"],
                                                    xn_mark])
                                mS = em.step("s", [
                                    (lambda it=it, b=bU1: (lambda e: e.activation(
                                        out=g[:, it * CH:(it + 1) * CH],
                                        in_=b["t"][:, 0:CH], func=AF.Silu)))()
                                ], sem="s", waits=[mU1])
                                banks.release(bU1, mS)
                                dct["lastU1"] = mU1
                                bU2 = banks.acquire()
                                mU2 = em.step("pe", [
                                    (lambda k, it=it, b=bU2, c=c, wB=wB: (lambda e: e.matmul(
                                        b["t"][:, 0:CH],
                                        wB[:, k * 1024 + it * 128:k * 1024 + (it + 1) * 128],
                                        xn_bf[:, k * NS + c * CH:k * NS + (c + 1) * CH],
                                        start=(k == 0), stop=(k == KD - 1))))(k)
                                    for k in range(KD)
                                ], sem="pe", waits=[(f"w{slB}", mB[1]), bU2["free"]])
                                mG = em.step("v", [
                                    (lambda it=it, b=bU2: (lambda e: e.tensor_tensor(
                                        out=g[:, it * CH:(it + 1) * CH],
                                        in0=g[:, it * CH:(it + 1) * CH],
                                        in1=b["t"][:, 0:CH], op=ALU.mult)))()
                                ], sem="v", waits=[mU2, mS])
                                banks.release(bU2, mG)
                                dct["lastU2"] = mU2
                                g_marks.append(mG)
                            for dd in range(KD):
                                bZ = banks.acquire()
                                mZ = em.step("pe", [
                                    (lambda it, dd=dd, b=bZ, wC=wC: (lambda e: e.matmul(
                                        b["t"][:, 0:CH],
                                        wC[:, it * 1024 + dd * 128:it * 1024 + (dd + 1) * 128],
                                        g[:, it * CH:(it + 1) * CH],
                                        start=(it == 0), stop=(it == KD - 1))))(it)
                                    for it in range(KD)
                                ], sem="pe", waits=[(f"w{slC}", mC[1]), bZ["free"]] + g_marks)
                                if ih == 0:
                                    ywait = prev_fence if st == 0 else [chunk_free[c]]
                                    mY = em.step("s", [
                                        (lambda dd=dd, b=bZ, c=c: (lambda e: e.activation(
                                            out=y32[:, dd * NS + c * CH:dd * NS + (c + 1) * CH],
                                            in_=b["t"][:, 0:CH], func=AF.Identity)))()
                                    ], sem="s", waits=[mZ] + ywait)
                                else:
                                    mY = em.step("v", [
                                        (lambda dd=dd, b=bZ, c=c: (lambda e: e.tensor_tensor(
                                            out=y32[:, dd * NS + c * CH:dd * NS + (c + 1) * CH],
                                            in0=y32[:, dd * NS + c * CH:dd * NS + (c + 1) * CH],
                                            in1=b["t"][:, 0:CH], op=ALU.add)))()
                                    ], sem="v", waits=[mZ])
                                banks.release(bZ, mY)
                                dct["lastZ"] = mZ

                        def dense_slabs(ih):
                            slA, mA = load_weight(
                                SW1.rearrange("(k p) i -> p k i", p=128)[:, :, ih * 1024:(ih + 1) * 1024])
                            slB, mB = load_weight(
                                SW2.rearrange("(k p) i -> p k i", p=128)[:, :, ih * 1024:(ih + 1) * 1024])
                            slC, mC = load_weight(
                                SW3[ih * 1024:(ih + 1) * 1024, :].rearrange("(k p) d -> p k d", p=128))
                            return ((slA, mA, slB, mB, slC, mC),
                                    (wslots[slA], wslots[slB], wslots[slC]))

                        sl0, wm0 = dense_slabs(0)
                        tr_prev = None
                        tr_marks = []
                        mks = []

                        def emit_trans(c):
                            nonlocal tr_prev
                            m_t = em.step("pe", [
                                (lambda tt, c=c: (lambda e: e.transpose(
                                    psT[:, tt * 128:(tt + 1) * 128],
                                    ssel_bf[:, (c * NTT + tt) * 8:(c * NTT + tt + 1) * 8],
                                    identb[:, :])))(tt)
                                for tt in range(NTT)
                            ], sem="pe", waits=[mks[c]["ssel"], tr_prev])
                            m_e = em.step("v", [
                                (lambda c=c: (lambda e: e.tensor_copy(
                                    out=sT_sb[0:8, c * CH:(c + 1) * CH],
                                    in_=psT[0:8, 0:CH])))()
                            ], sem="v", waits=[m_t])
                            tr_prev = m_e
                            tr_marks.append(m_e)

                        for c in range(NCH):
                            t0 = tok0 + c * CH
                            if st == 0:
                                pm = list(prev_fence) if c < 2 else []
                            else:
                                pm = [chunk_free[c]]
                            mk = emit_ln_chunk(
                                srcr[:, :, t0:t0 + CH],
                                (lambda k, c=c: xn_bf[:, k * NS + c * CH:k * NS + (c + 1) * CH]),
                                routing,
                                (lambda tt, c=c: ssel_bf[:, (c * NTT + tt) * 8:(c * NTT + tt + 1) * 8]),
                                lb, pm, banks, c)
                            mks.append(mk)
                            if c >= 1:
                                dense_chunk(c - 1, 0, sl0, wm0, mks[c - 1]["xn"])
                                emit_trans(c - 1)
                        dense_chunk(NCH - 1, 0, sl0, wm0, mks[NCH - 1]["xn"])
                        emit_trans(NCH - 1)
                        wunits["free"][sl0[0]] = [dct["lastU1"]]
                        wunits["free"][sl0[2]] = [dct["lastU2"]]
                        wunits["free"][sl0[4]] = [dct["lastZ"]]
                        sl1, wm1 = dense_slabs(1)
                        for c in range(NCH):
                            dense_chunk(c, 1, sl1, wm1, None)
                        wunits["free"][sl1[0]] = [dct["lastU1"]]
                        wunits["free"][sl1[2]] = [dct["lastU2"]]
                        wunits["free"][sl1[4]] = [dct["lastZ"]]
                        dense_done = dct["lastZ"]
                    sb_reader = {0: None, 1: None}
                    sb_mark = {}

                    def sb_build(ee):
                        par = ee % 2
                        marks = []
                        for c in range(NCH):
                            bS = banks.acquire()
                            mS1 = em.step("pe", [
                                (lambda c=c, ee=ee, b=bS: (lambda e: e.matmul(
                                    b["t"][:, 0:CH],
                                    sel_all[0:8, ee * 128:(ee + 1) * 128],
                                    sT_sb[0:8, c * CH:(c + 1) * CH],
                                    start=True, stop=True)))()
                            ], sem="pe", waits=[bS["free"]] + tr_marks)
                            mS2 = em.step("v", [
                                (lambda c=c, par=par, b=bS: (lambda e: e.tensor_copy(
                                    out=s_b[:, par * NS + c * CH:par * NS + (c + 1) * CH],
                                    in_=b["t"][:, 0:CH])))()
                            ], sem="v", waits=[mS1, sb_reader[par]])
                            banks.release(bS, mS2)
                            marks.append(mS2)
                        sb_mark[ee] = marks

                    sb_build(0)
                    last_y = None
                    y_done = [None] * NCH
                    st_mk = {0: None, 1: None}
                    m_o_marks = []

                    def emit_res_chunk(c):
                        t0 = tok0 + c * CH
                        par = c % 2
                        xr = lb["x32c"][par]
                        lsem = "xla" if par == 0 else "xlb"
                        ssem = "sta" if par == 0 else "stb"
                        m_lx = em.step("sy", [
                            (lambda t0=t0, xr=xr: (lambda e: e.dma_start(
                                out=xr[:, :].rearrange("p (k n) -> p k n", k=KD),
                                in_=srcr[:, :, t0:t0 + CH])))()
                        ], sem=lsem, inc=16,
                            waits=[st_mk[par]] + lb["free"][par])
                        m_o = em.step("v", [
                            (lambda k, c=c, xr=xr: (lambda e: e.scalar_tensor_tensor(
                                out=xr[:, k * CH:(k + 1) * CH],
                                in0=y32[:, k * NS + c * CH:k * NS + (c + 1) * CH],
                                scalar=0.5,
                                in1=xr[:, k * CH:(k + 1) * CH],
                                op0=ALU.mult, op1=ALU.add)))(k)
                            for k in range(KD)
                        ], sem="v", waits=[m_lx, y_done[c]])
                        m_o_marks.append(m_o)
                        m_st = em.step("sy", [
                            (lambda t0=t0, xr=xr: (lambda e: e.dma_start(
                                out=dstr[:, :, t0:t0 + CH],
                                in_=xr[:, :].rearrange("p (k n) -> p k n", k=KD))))()
                        ], sem=ssem, inc=16, waits=[m_o])
                        st_mk[par] = m_st
                        lb["free"][par] = [m_st]

                    with contextlib.ExitStack() as S3:
                        SE = S3.enter_context
                        u = SE(nc.sbuf_tensor(f"{tag}{st}_u", [128, KD * CH], BF16))
                        h = SE(nc.sbuf_tensor(f"{tag}{st}_h", [128, KD * CH], BF16))
                        x3s = SE(nc.sbuf_tensor(f"{tag}{st}_x3", [128, KD * CH], BF16))
                        p = SE(nc.sbuf_tensor(f"{tag}{st}_p", [128, KD * CH // 2], F32))
                        wms = {}

                        def load_expert(ee):
                            sl1, m1 = load_weight(W1[ee].rearrange("(k p) h -> p k h", p=128))
                            sl3, m3 = load_weight(W3[ee].rearrange("(k p) h -> p k h", p=128))
                            sl2, m2 = load_weight(W2[ee].rearrange("(k p) d -> p k d", p=128))
                            wms[ee] = (sl1, m1, sl3, m3, sl2, m2,
                                       wslots[sl1], wslots[sl3], wslots[sl2])

                        load_expert(0)

                        def emit_u(ee, c, extra):
                            par = ee % 2
                            return em.step("v", [
                                (lambda k, c=c, par=par: (lambda e: e.tensor_tensor(
                                    out=u[:, k * CH:(k + 1) * CH],
                                    in0=xn_bf[:, k * NS + c * CH:k * NS + (c + 1) * CH],
                                    in1=s_b[:, par * NS + c * CH:par * NS + (c + 1) * CH],
                                    op=ALU.mult)))(k)
                                for k in range(KD)
                            ], sem="v", waits=[sb_mark[ee][c]] + extra)

                        u_mark = emit_u(0, 0, [dense_done])
                        m1_last = [None]
                        for ee in range(E):
                            sl1, m1, sl3, m3, sl2, m2, wt1, wt3, wt2 = wms[ee]
                            par = ee % 2
                            for c in range(NCH):
                                m_u = u_mark
                                h_marks = []
                                for ht in range(KD):
                                    b = banks.acquire()
                                    mM = em.step("pe", [
                                        (lambda k, ht=ht, b=b, w=wt1: (lambda e: e.matmul(
                                            b["t"][:, 0:CH],
                                            w[:, k * 1024 + ht * 128:k * 1024 + (ht + 1) * 128],
                                            u[:, k * CH:(k + 1) * CH],
                                            start=(k == 0), stop=(k == KD - 1))))(k)
                                        for k in range(KD)
                                    ], sem="pe", waits=[(f"w{sl1}", m1[1]), m_u, b["free"]])
                                    mH = em.step("s", [
                                        (lambda ht=ht, b=b, ee=ee: (lambda e: e.activation(
                                            out=h[:, ht * CH:(ht + 1) * CH],
                                            in_=b["t"][:, 0:CH], func=AF.Silu,
                                            bias=b1s[:, ht * E + ee:ht * E + ee + 1],
                                            scale=1.0)))()
                                    ], sem="s", waits=[mM])
                                    banks.release(b, mH)
                                    h_marks.append(mH)
                                m1_done_c = em.mark("pe")
                                x3_marks = []
                                for ht in range(KD):
                                    b = banks.acquire()
                                    mM = em.step("pe", [
                                        (lambda k, ht=ht, b=b, w=wt3: (lambda e: e.matmul(
                                            b["t"][:, 0:CH],
                                            w[:, k * 1024 + ht * 128:k * 1024 + (ht + 1) * 128],
                                            u[:, k * CH:(k + 1) * CH],
                                            start=(k == 0), stop=(k == KD - 1))))(k)
                                        for k in range(KD)
                                    ], sem="pe", waits=[(f"w{sl3}", m3[1]), b["free"]])
                                    mX3 = em.step("s", [
                                        (lambda ht=ht, b=b, ee=ee: (lambda e: e.activation(
                                            out=x3s[:, ht * CH:(ht + 1) * CH],
                                            in_=b["t"][:, 0:CH], func=AF.Identity,
                                            bias=b3s[:, ht * E + ee:ht * E + ee + 1],
                                            scale=1.0)))()
                                    ], sem="s", waits=[mM])
                                    banks.release(b, mX3)
                                    x3_marks.append(mX3)
                                m3_done = em.mark("pe")
                                if c == NCH - 1:
                                    m1_last[0] = m1_done_c
                                if c == 0 and ee + 1 < E:
                                    sb_build(ee + 1)
                                if c + 1 < NCH:
                                    u_mark = emit_u(ee, c + 1, [m3_done])
                                elif ee + 1 < E:
                                    u_mark = emit_u(ee + 1, 0, [m3_done])
                                m_y = None
                                for wave in range(2):
                                    p_marks = []
                                    for dd in range(wave * 4, wave * 4 + 4):
                                        b = banks.acquire()
                                        mM = em.step("pe", [
                                            (lambda k, dd=dd, b=b, w=wt2: (lambda e: e.matmul(
                                                b["t"][:, 0:CH],
                                                w[:, k * 1024 + dd * 128:k * 1024 + (dd + 1) * 128],
                                                h[:, k * CH:(k + 1) * CH],
                                                start=(k == 0), stop=(k == KD - 1))))(k)
                                            for k in range(KD)
                                        ], sem="pe", waits=[(f"w{sl2}", m2[1]), b["free"]] + h_marks)
                                        mP = em.step("v", [
                                            (lambda dd=dd, b=b, ee=ee: (lambda e: e.scalar_tensor_tensor(
                                                out=p[:, (dd % 4) * CH:(dd % 4 + 1) * CH],
                                                in0=b["t"][:, 0:CH],
                                                scalar=b2s[:, dd * E + ee:dd * E + ee + 1],
                                                in1=x3s[:, dd * CH:(dd + 1) * CH],
                                                op0=ALU.add, op1=ALU.mult)))()
                                        ], sem="v", waits=[mM, m_y if wave else last_y] + x3_marks)
                                        banks.release(b, mP)
                                        p_marks.append(mP)
                                    em.ewait("v", p_marks[-1])
                                    m_pm = em.step("v", [
                                        (lambda dd, c=c, par=par: (lambda e: e.scalar_tensor_tensor(
                                            out=p[:, (dd % 4) * CH:(dd % 4 + 1) * CH],
                                            in0=s_b[:, par * NS + c * CH:par * NS + (c + 1) * CH],
                                            scalar=0.0,
                                            in1=p[:, (dd % 4) * CH:(dd % 4 + 1) * CH],
                                            op0=ALU.is_gt, op1=ALU.mult)))(dd)
                                        for dd in range(wave * 4, wave * 4 + 4)
                                    ], sem="v", waits=p_marks)
                                    em.ewait("v", m_pm)
                                    m_y = em.step("v", [
                                        (lambda dd, c=c: (lambda e: e.tensor_tensor(
                                            out=y32[:, dd * NS + c * CH:dd * NS + (c + 1) * CH],
                                            in0=y32[:, dd * NS + c * CH:dd * NS + (c + 1) * CH],
                                            in1=p[:, (dd % 4) * CH:(dd % 4 + 1) * CH],
                                            op=ALU.add)))(dd)
                                        for dd in range(wave * 4, wave * 4 + 4)
                                    ], sem="v", waits=[m_pm])
                                last_y = m_y
                                if ee == E - 1:
                                    y_done[c] = m_y
                                    emit_res_chunk(c)
                                if c == NCH - 1:
                                    sb_reader[par] = m_pm
                            wunits["free"][sl1] = [m1_last[0]]
                            wunits["free"][sl3] = [m3_done]
                            wunits["free"][sl2] = [em.mark("pe")]
                            if ee + 1 < E:
                                load_expert(ee + 1)
                        exp_fence = em.fence(engs=("pe", "v", "s"))
                    chunk_free = m_o_marks
                    prev_fence = em.fence(engs=("pe", "v", "s"),
                                          dma=("sta", "stb"))
            return em.fence(engs=("pe", "v", "s"), dma=("sta", "stb"))

        def emit_attn(SRC, DST, ffn_fence):
            srcr = SRC.rearrange("(k p) t -> p k t", p=128)
            dstr = DST.rearrange("(k p) t -> p k t", p=128)
            NQC = T // CH
            NJT = T // 128
            banks = BankPool(em, [ps0, ps1, ps2, ps3])
            with contextlib.ExitStack() as AS:
                AE = AS.enter_context
                keff = AE(nc.sbuf_tensor("a_keff", [128, KD * L], BF16))
                vM = AE(nc.sbuf_tensor("a_vM", [128, 2 * D], BF16))
                wdkvt = AE(nc.sbuf_tensor("a_wdkvt", [128, KD * L], BF16))
                wdqt = AE(nc.sbuf_tensor("a_wdqt", [128, KD * L], BF16))
                wuqt = AE(nc.sbuf_tensor("a_wuqt", [128, 2 * D], BF16))
                masks = AE(nc.sbuf_tensor("a_masks", [128, 4 * CH], BF16))
                JSC = contextlib.ExitStack()
                jm2 = JSC.enter_context(nc.sbuf_tensor("a_jm2", [128, CH], I32))
                m_mask = []
                for v in range(4):
                    mi = em.step("gp", [
                        (lambda v=v: (lambda e: e.iota(jm2[:, :], [[1, CH]],
                                                       base=-128 * v,
                                                       channel_multiplier=-1)))()
                    ], sem="gp", waits=[m_mask[-1] if m_mask else None] + ffn_fence)
                    mv = em.step("v", [
                        (lambda v=v: (lambda e: e.tensor_scalar(
                            out=masks[:, v * CH:(v + 1) * CH], in0=jm2[:, :],
                            scalar1=0.0, scalar2=None, op0=ALU.is_ge)))()
                    ], sem="v", waits=[mi] + ffn_fence)
                    m_mask.append(mv)
                JSC.close()
                m_aw = em.step("gp", [
                    lambda e: e.dma_start(out=wdkvt[:, :].rearrange("p (k n) -> p k n", k=KD),
                                          in_=WDKVT.rearrange("(k p) n -> p k n", p=128)),
                    lambda e: e.dma_start(out=wdqt[:, :].rearrange("p (k n) -> p k n", k=KD),
                                          in_=WDQT.rearrange("(k p) n -> p k n", p=128)),
                    lambda e: e.dma_start(out=wuqt[:, :].rearrange("p (k n) -> p k n", k=2),
                                          in_=WUQT.rearrange("(k p) n -> p k n", p=128)),
                ], sem="aw", inc=16, inc_each=True, waits=ffn_fence)
                with contextlib.ExitStack() as PS:
                    PEx = PS.enter_context
                    t1t = PEx(nc.sbuf_tensor("a_t1t", [128, KD * D], BF16))
                    wot = PEx(nc.sbuf_tensor("a_wot", [128, KD * D], BF16))
                    wdq = PEx(nc.sbuf_tensor("a_wdq", [128, 2 * D], BF16))
                    wuk = PEx(nc.sbuf_tensor("a_wuk", [128, KD * L], BF16))
                    wuv = PEx(nc.sbuf_tensor("a_wuv", [128, KD * L], BF16))
                    m_aw2 = em.step("gp", [
                        lambda e: e.dma_start(out=wot[:, :].rearrange("p (k n) -> p k n", k=KD),
                                              in_=WOT.rearrange("(k p) n -> p k n", p=128)),
                        lambda e: e.dma_start(out=wdq[:, :].rearrange("p (k n) -> p k n", k=2),
                                              in_=WDQ.rearrange("(k p) n -> p k n", p=128)),
                        lambda e: e.dma_start(out=wuk[:, :].rearrange("p (k n) -> p k n", k=KD),
                                              in_=WUK.rearrange("(k p) n -> p k n", p=128)),
                        lambda e: e.dma_start(out=wuv[:, :].rearrange("p (k n) -> p k n", k=KD),
                                              in_=WUV.rearrange("(k p) n -> p k n", p=128)),
                    ], sem="aw", inc=16, inc_each=True, waits=ffn_fence)
                    AWALL = ("aw", 112)
                    t1_marks = []
                    for ct in range(KD):
                        for half in range(2):
                            b = banks.acquire()
                            mm = em.step("pe", [
                                (lambda k, ct=ct, half=half, b=b: (lambda e: e.matmul(
                                    b["t"][:, 0:CH],
                                    wuqt[:, k * D + ct * 128:k * D + (ct + 1) * 128],
                                    wdq[:, k * D + half * CH:k * D + (half + 1) * CH],
                                    start=(k == 0), stop=(k == 1))))(k)
                                for k in range(2)
                            ], sem="pe", waits=[AWALL, b["free"]] + ffn_fence)
                            me = em.step("v", [
                                (lambda ct=ct, half=half, b=b: (lambda e: e.tensor_copy(
                                    out=t1t[:, ct * D + half * CH:ct * D + (half + 1) * CH],
                                    in_=b["t"][:, 0:CH])))()
                            ], sem="v", waits=[mm] + m_mask)
                            banks.release(b, me)
                            t1_marks.append(me)
                    keff_marks = []
                    for ct in range(KD):
                        b = banks.acquire()
                        mm = em.step("pe", [
                            (lambda k, ct=ct, b=b: (lambda e: e.matmul(
                                b["t"][:, 0:L],
                                t1t[:, k * D + ct * 128:k * D + (ct + 1) * 128],
                                wuk[:, k * L:(k + 1) * L],
                                start=(k == 0), stop=(k == KD - 1))))(k)
                            for k in range(KD)
                        ], sem="pe", waits=[b["free"]] + t1_marks)
                        me = em.step("v", [
                            (lambda ct=ct, b=b: (lambda e: e.tensor_copy(
                                out=keff[:, ct * L:(ct + 1) * L], in_=b["t"][:, 0:L])))()
                        ], sem="v", waits=[mm])
                        banks.release(b, me)
                        keff_marks.append(me)
                    vm_marks = []
                    for lt in range(2):
                        for half in range(2):
                            b = banks.acquire()
                            mm = em.step("pe", [
                                (lambda k, lt=lt, half=half, b=b: (lambda e: e.matmul(
                                    b["t"][:, 0:CH],
                                    wuv[:, k * L + lt * 128:k * L + (lt + 1) * 128],
                                    wot[:, k * D + half * CH:k * D + (half + 1) * CH],
                                    start=(k == 0), stop=(k == KD - 1))))(k)
                                for k in range(KD)
                            ], sem="pe", waits=[b["free"]])
                            me = em.step("v", [
                                (lambda lt=lt, half=half, b=b: (lambda e: e.tensor_copy(
                                    out=vM[:, lt * D + half * CH:lt * D + (half + 1) * CH],
                                    in_=b["t"][:, 0:CH])))()
                            ], sem="v", waits=[mm])
                            banks.release(b, me)
                            vm_marks.append(me)
                    prep_fence = em.fence(engs=("pe", "v"))
                xn = AE(nc.sbuf_tensor("a_xn", [128, KD * T], BF16))
                qf = AE(nc.sbuf_tensor("a_qf", [128, KD * T], BF16))
                ya = AE(nc.sbuf_tensor("a_ya", [128, KD * T], BF16))
                ckf = AE(nc.sbuf_tensor("a_ckf", [128, 2 * T], BF16))
                ckt = AE(nc.sbuf_tensor("a_ckt", [128, NJT * L], BF16))
                qlat = AE(nc.sbuf_tensor("a_qlat", [128, 2 * T], BF16))
                Et = AE(nc.sbuf_tensor("a_Et", [128, 4 * CH], BF16))
                ctxs = AE(nc.sbuf_tensor("a_ctxs", [128, 2 * CH], BF16))
                rbf = AE(nc.sbuf_tensor("a_rbf", [1, CH], BF16))
                rln = AE(nc.sbuf_tensor("a_rln", [1, CH], F32))
                rbs = AE(nc.sbuf_tensor("a_rbs", [64, CH], F32))
                lba = mk_ln_bufs(AE, "aln", nbuf=2)
                seq_fence = prep_fence
                st_mk = {0: None, 1: None}
                et_free = [None, None, None, None]
                score_free = [None, None, None]
                pending_epi = [None, None]
                EPI = {"prev_rln": None, "prev_rbe": None, "prev_ymm": None,
                       "prev_ya": None, "sum_done": None, "ctx_done": None,
                       "prev_r": None, "prev_rbmm": None}
                for sq in range(NSEQ):
                    q0 = sq * T
                    xn_marks = []
                    ck_marks = []
                    qf_marks = []
                    mks = []
                    with contextlib.ExitStack() as QS:
                        SE = QS.enter_context
                        qlow = SE(nc.sbuf_tensor(f"as{sq}_qlow", [128, 2 * CH], BF16))
                        qfl = {"mm": None}

                        def proj_chunk(c):
                            xnm = mks[c]["xn"]
                            for lh in range(2):
                                b = banks.acquire()
                                mm = em.step("pe", [
                                    (lambda k, lh=lh, c=c, b=b: (lambda e: e.matmul(
                                        b["t"][:, 0:CH],
                                        wdkvt[:, k * L + lh * 128:k * L + (lh + 1) * 128],
                                        xn[:, k * T + c * CH:k * T + (c + 1) * CH],
                                        start=(k == 0), stop=(k == KD - 1))))(k)
                                    for k in range(KD)
                                ], sem="pe", waits=[b["free"], AWALL, xnm])
                                me = em.step("v", [
                                    (lambda lh=lh, c=c, b=b: (lambda e: e.tensor_copy(
                                        out=ckf[:, lh * T + c * CH:lh * T + (c + 1) * CH],
                                        in_=b["t"][:, 0:CH])))()
                                ], sem="v", waits=[mm])
                                banks.release(b, me)
                                ck_marks.append(me)
                            for jt in range(4 * c, 4 * c + 4):
                                b = banks.acquire()
                                mm = em.step("pe", [
                                    (lambda k, jt=jt, b=b: (lambda e: e.matmul(
                                        b["t"][:, 0:L],
                                        xn[:, k * T + jt * 128:k * T + (jt + 1) * 128],
                                        wdkvt[:, k * L:(k + 1) * L],
                                        start=(k == 0), stop=(k == KD - 1))))(k)
                                    for k in range(KD)
                                ], sem="pe", waits=[b["free"], xnm])
                                me = em.step("v", [
                                    (lambda jt=jt, b=b: (lambda e: e.tensor_copy(
                                        out=ckt[:, jt * L:(jt + 1) * L], in_=b["t"][:, 0:L])))()
                                ], sem="v", waits=[mm])
                                banks.release(b, me)
                                ck_marks.append(me)
                            ql_marks = []
                            for lh in range(2):
                                b = banks.acquire()
                                mm = em.step("pe", [
                                    (lambda k, lh=lh, c=c, b=b: (lambda e: e.matmul(
                                        b["t"][:, 0:CH],
                                        wdqt[:, k * L + lh * 128:k * L + (lh + 1) * 128],
                                        xn[:, k * T + c * CH:k * T + (c + 1) * CH],
                                        start=(k == 0), stop=(k == KD - 1))))(k)
                                    for k in range(KD)
                                ], sem="pe", waits=[b["free"], AWALL, xnm])
                                me = em.step("v", [
                                    (lambda lh=lh, b=b: (lambda e: e.tensor_copy(
                                        out=qlow[:, lh * CH:(lh + 1) * CH],
                                        in_=b["t"][:, 0:CH])))()
                                ], sem="v", waits=[mm, qfl["mm"]])
                                banks.release(b, me)
                                ql_marks.append(me)
                            for ct in range(KD):
                                b = banks.acquire()
                                mm = em.step("pe", [
                                    (lambda k, ct=ct, b=b: (lambda e: e.matmul(
                                        b["t"][:, 0:CH],
                                        wuqt[:, k * D + ct * 128:k * D + (ct + 1) * 128],
                                        qlow[:, k * CH:(k + 1) * CH],
                                        start=(k == 0), stop=(k == 1))))(k)
                                    for k in range(2)
                                ], sem="pe", waits=[b["free"]] + ql_marks)
                                me = em.step("v", [
                                    (lambda ct=ct, c=c, b=b: (lambda e: e.tensor_copy(
                                        out=qf[:, ct * T + c * CH:ct * T + (c + 1) * CH],
                                        in_=b["t"][:, 0:CH])))()
                                ], sem="v", waits=[mm])
                                banks.release(b, me)
                                qf_marks.append(me)
                            qfl["mm"] = em.mark("pe")

                        for c in range(NQC):
                            t0 = q0 + c * CH
                            if sq == 0:
                                pm = list(seq_fence) if c < 2 else []
                            else:
                                pm = []
                            mk = emit_ln_chunk(
                                srcr[:, :, t0:t0 + CH],
                                (lambda k, c=c: xn[:, k * T + c * CH:k * T + (c + 1) * CH]),
                                None, None, lba, pm, banks, c)
                            mks.append(mk)
                            xn_marks.append(mk["xn"])
                            if c >= 1:
                                proj_chunk(c - 1)
                        proj_chunk(NQC - 1)
                    qpool = BankPool(em, [ps0, ps1, ps2])
                    ya_done = {}

                    def emit_res_chunk_a(c):
                        t0 = q0 + c * CH
                        par = c % 2
                        xr = lba["x32c"][par]
                        lsem = "xla" if par == 0 else "xlb"
                        ssem = "sta" if par == 0 else "stb"
                        m_lx = em.step("sy", [
                            (lambda t0=t0, xr=xr: (lambda e: e.dma_start(
                                out=xr[:, :].rearrange("p (k n) -> p k n", k=KD),
                                in_=srcr[:, :, t0:t0 + CH])))()
                        ], sem=lsem, inc=16,
                            waits=[st_mk[par]] + lba["free"][par])
                        m_o = em.step("v", [
                            (lambda k, c=c, xr=xr: (lambda e: e.tensor_tensor(
                                out=xr[:, k * CH:(k + 1) * CH],
                                in0=ya[:, k * T + c * CH:k * T + (c + 1) * CH],
                                in1=xr[:, k * CH:(k + 1) * CH], op=ALU.add)))(k)
                            for k in range(KD)
                        ], sem="v", waits=[m_lx, ya_done[c]])
                        m_st = em.step("sy", [
                            (lambda t0=t0, xr=xr: (lambda e: e.dma_start(
                                out=dstr[:, :, t0:t0 + CH],
                                in_=xr[:, :].rearrange("p (k n) -> p k n", k=KD))))()
                        ], sem=ssem, inc=16, waits=[m_o])
                        st_mk[par] = m_st
                        lba["free"][par] = [m_st]
                    for hh in range(NH):
                        ct_h = hh // 2
                        po_h = (hh % 2) * 64
                        qlat_marks = []
                        for lh in range(2):
                            for c in range(NQC):
                                b = qpool.acquire()
                                mm = em.step("pe", [
                                    (lambda lh=lh, c=c, b=b, ct_h=ct_h, po_h=po_h:
                                     (lambda e: e.matmul(
                                         b["t"][:, 0:CH],
                                         keff[po_h:po_h + 64,
                                              ct_h * L + lh * 128:ct_h * L + (lh + 1) * 128],
                                         qf[po_h:po_h + 64,
                                            ct_h * T + c * CH:ct_h * T + (c + 1) * CH],
                                         start=True, stop=True)))()
                                ], sem="pe", waits=[b["free"], score_free[0],
                                                    score_free[1], score_free[2]] +
                                    qf_marks + keff_marks)
                                me = em.step("v", [
                                    (lambda lh=lh, c=c, b=b: (lambda e: e.tensor_copy(
                                        out=qlat[:, lh * T + c * CH:lh * T + (c + 1) * CH],
                                        in_=b["t"][:, 0:CH])))()
                                ], sem="v", waits=[mm])
                                qpool.release(b, me)
                                qlat_marks.append(me)
                        for qc in range(NQC):
                            njt = 4 * (qc + 1)
                            last_e = {}
                            for j in range(njt + 2):
                                if j < njt:
                                    sb = (ps0, ps1, ps2)[j % 3]
                                    m_sc = em.step("pe", [
                                        (lambda lh, j=j, qc=qc, sb=sb: (lambda e: e.matmul(
                                            sb[:, 0:CH],
                                            ckf[:, lh * T + j * 128:lh * T + (j + 1) * 128],
                                            qlat[:, lh * T + qc * CH:lh * T + (qc + 1) * CH],
                                            start=(lh == 0), stop=(lh == 1))))(lh)
                                        for lh in range(2)
                                    ], sem="pe", waits=qlat_marks + ck_marks +
                                        [score_free[j % 3], et_free[j % 4]])
                                if j == 1 and pending_epi[0] is not None:
                                    pending_epi[0]()
                                    pending_epi[0] = None
                                if j == 3 and pending_epi[1] is not None:
                                    pending_epi[1]()
                                    pending_epi[1] = None
                                if j < njt:
                                    m_es = em.step("s", [
                                        (lambda j=j, sb=sb: (lambda e: e.activation(
                                            out=Et[:, (j % 4) * CH:(j % 4 + 1) * CH],
                                            in_=sb[:, 0:CH], func=AF.Exp, scale=0.125)))()
                                    ], sem="s", waits=[m_sc, et_free[j % 4]])
                                    score_free[j % 3] = m_es
                                    m_e = m_es
                                    if j >= 4 * qc:
                                        vv = j - 4 * qc
                                        m_e = em.step("v", [
                                            (lambda j=j, vv=vv: (lambda e: e.tensor_tensor(
                                                out=Et[:, (j % 4) * CH:(j % 4 + 1) * CH],
                                                in0=Et[:, (j % 4) * CH:(j % 4 + 1) * CH],
                                                in1=masks[:, vv * CH:(vv + 1) * CH],
                                                op=ALU.mult)))()
                                        ], sem="v", waits=[m_e] + m_mask)
                                    last_e[j] = m_e
                                jj = j - 2
                                if jj >= 0:
                                    m_cx = em.step("pe", [
                                        (lambda lh, jj=jj, njt=njt: (lambda e: e.matmul(
                                            (psA if lh == 0 else psB)[:, 0:CH],
                                            ckt[:, jj * L + lh * 128:jj * L + (lh + 1) * 128],
                                            Et[:, (jj % 4) * CH:(jj % 4 + 1) * CH],
                                            start=(jj == 0), stop=(jj == njt - 1))))(lh)
                                        for lh in range(2)
                                    ] + [
                                        (lambda jj=jj, njt=njt: (lambda e: e.matmul(
                                            psS[0:1, 0:CH], onesc[:, :],
                                            Et[:, (jj % 4) * CH:(jj % 4 + 1) * CH],
                                            start=(jj == 0), stop=(jj == njt - 1))))(jj)
                                    ], sem="pe", waits=[last_e[jj]] +
                                        ([EPI["ctx_done"], EPI["sum_done"]] if jj == 0 else []))
                                    et_free[jj % 4] = m_cx
                            pe_tail = em.mark("pe")

                            def epiA(pe_tail=pe_tail):
                                m_rl = em.step("s", [
                                    lambda e: e.activation(out=rln[0:1, :],
                                                           in_=psS[0:1, 0:CH],
                                                           func=AF.Ln, scale=1.0),
                                ], sem="s", waits=[pe_tail, EPI["prev_rln"]])
                                m_r = em.step("s", [
                                    lambda e: e.activation(out=rbf[0:1, :],
                                                           in_=rln[0:1, :],
                                                           func=AF.Exp, scale=-1.0),
                                ], sem="s", waits=[m_rl, EPI["prev_rbmm"]])
                                m_cxe = em.step("v", [
                                    (lambda lh: (lambda e: e.tensor_copy(
                                        out=ctxs[:, lh * CH:(lh + 1) * CH],
                                        in_=(psA if lh == 0 else psB)[:, 0:CH])))(lh)
                                    for lh in range(2)
                                ], sem="v", waits=[pe_tail, EPI["prev_ymm"]])
                                EPI.update(prev_rln=m_r, prev_r=m_r,
                                           sum_done=m_rl, ctx_done=m_cxe)

                            def epiB(hh=hh, qc=qc, ct_h=ct_h, po_h=po_h):
                                m_r = EPI["prev_r"]
                                m_cxe = EPI["ctx_done"]
                                m_rb = em.step("pe", [
                                    lambda e: e.matmul(ps3[64:128, 0:CH],
                                                       ones1b[0:1, 0:64],
                                                       rbf[0:1, :], start=True,
                                                       stop=True),
                                ], sem="pe", waits=[m_r, EPI["prev_rbe"],
                                                    EPI["prev_ya"]])
                                EPI["prev_rbmm"] = m_rb
                                m_rbe = em.step("v", [
                                    lambda e: e.tensor_copy(out=rbs[0:64, :],
                                                            in_=ps3[64:128, 0:CH]),
                                ], sem="v", waits=[m_rb, EPI["prev_ya"]])
                                m_y = em.step("pe", [
                                    (lambda lh, hh=hh: (lambda e: e.matmul(
                                        ps3[0:64, 0:CH],
                                        vM[:, lh * D + hh * 64:lh * D + (hh + 1) * 64],
                                        ctxs[:, lh * CH:(lh + 1) * CH],
                                        start=(lh == 0), stop=(lh == 1))))(lh)
                                    for lh in range(2)
                                ], sem="pe", waits=[m_cxe, m_rbe, EPI["prev_ya"]] + vm_marks)
                                m_ye = em.step("v", [
                                    (lambda hh=hh, qc=qc, ct_h=ct_h, po_h=po_h:
                                     (lambda e: e.tensor_tensor(
                                         out=ya[po_h:po_h + 64,
                                                ct_h * T + qc * CH:ct_h * T + (qc + 1) * CH],
                                         in0=ps3[0:64, 0:CH], in1=rbs[0:64, :],
                                         op=ALU.mult)))()
                                ], sem="v", waits=[m_y])
                                EPI.update(prev_rbe=m_rbe, prev_ymm=m_y,
                                           prev_ya=m_ye)
                                if hh == NH - 1:
                                    ya_done[qc] = m_ye
                                    emit_res_chunk_a(qc)
                            pending_epi[0] = epiA
                            pending_epi[1] = epiB
                    for pi in range(2):
                        if pending_epi[pi] is not None:
                            pending_epi[pi]()
                            pending_epi[pi] = None
                    banks.banks[0]["free"] = [score_free[0]]
                    banks.banks[1]["free"] = [score_free[1]]
                    banks.banks[2]["free"] = [score_free[2]]
                    banks.banks[3]["free"] = [EPI["prev_rbe"], EPI["prev_ya"]]
            return em.fence(engs=("pe", "v", "s"), dma=("sta", "stb"))

        f1 = emit_ffn(X0, XB1, "f1", None)
        a1 = emit_attn(XB1, XB2, f1)
        f2 = emit_ffn(XB2, OUT, "f2", a1)
        em.ewait("sy", em.mark("sta"))
        em.ewait("sy", em.mark("stb"))

        with nc.Block() as block:
            @block.tensor
            def _(t):
                for f in em.q["pe"]:
                    f(t)

            @block.vector
            def _(v):
                for f in em.q["v"]:
                    f(v)

            @block.scalar
            def _(s):
                for f in em.q["s"]:
                    f(s)

            @block.gpsimd
            def _(g):
                for f in em.q["gp"]:
                    f(g)

            @block.sync
            def _(sy):
                for f in em.q["sy"]:
                    f(sy)

    return nc


def _prep_weights(inp):
    c = np.ascontiguousarray
    w = {}
    w["W1"] = c(np.asarray(inp["L0_ffn_w1"]).transpose(0, 2, 1))
    w["W2"] = c(np.asarray(inp["L0_ffn_w2"]).transpose(0, 2, 1))
    w["W3"] = c(np.asarray(inp["L0_ffn_w3"]).transpose(0, 2, 1))
    w["SW1"] = c(np.asarray(inp["L0_ffn_sw1"]).T)
    w["SW2"] = c(np.asarray(inp["L0_ffn_sw2"]).T)
    w["SW3"] = c(np.asarray(inp["L0_ffn_sw3"]).T)
    for nm, src in (("B1", "L0_ffn_b1"), ("B2", "L0_ffn_b2"), ("B3", "L0_ffn_b3")):
        w[nm] = c(np.asarray(inp[src]).reshape(E, KD, 128).transpose(2, 1, 0))
    w["GW"] = c(np.asarray(inp["L0_ffn_gate_w"]).T)
    w["WDQ"] = c(np.asarray(inp["L0_attn_W_dq"]))
    w["WDQT"] = c(np.asarray(inp["L0_attn_W_dq"]).T)
    w["WUQT"] = c(np.asarray(inp["L0_attn_W_uq"]).T)
    w["WDKVT"] = c(np.asarray(inp["L0_attn_W_dkv"]).T)
    w["WUK"] = c(np.asarray(inp["L0_attn_W_uk"]))
    w["WUV"] = c(np.asarray(inp["L0_attn_W_uv"]))
    w["WOT"] = c(np.asarray(inp["L0_attn_W_o"]).T)
    return w


def kernel(**inputs):
    NSEQ, T = 4, 2048
    n_cores = 8
    x = np.asarray(inputs["L0_attn_x"])
    w = _prep_weights(inputs)
    nc = build_program(NSEQ=NSEQ, T=T)
    in_maps = []
    for ci in range(n_cores):
        xs = x[ci * NSEQ:(ci + 1) * NSEQ].reshape(NSEQ * T, D)
        m = {"X0": np.ascontiguousarray(xs.T)}
        m.update(w)
        in_maps.append(m)
    res = run_bass_kernel_spmd(nc, in_maps, core_ids=list(range(n_cores)))
    outs = []
    for ci in range(n_cores):
        o = np.asarray(res.results[ci]["OUT"])
        outs.append(o.T.reshape(NSEQ, T, D))
    return np.concatenate(outs, axis=0)
